# version 07
# baseline (speedup 1.0000x reference)
import numpy as np
from contextlib import ExitStack

import concourse.bass as bass
import concourse.mybir as mybir
from concourse.bass_utils import run_bass_kernel_spmd

F32 = mybir.dt.float32
BF16 = mybir.dt.bfloat16
AF = mybir.ActivationFunctionType
ALU = mybir.AluOpType

NCORES = 8
D = 2048
DI = 4096
NST = 128
H = 64
HBCW = 4416
KC = D // 128
CH = 128
HFF = 8192
EPS_LN = 1e-5
EPS_FFN = 1e-6

SAFE_EDGES = False
FENCE_EVERY = set()
DEBUG_TAPS = set()
STOP_AFTER = None


class _StopBuild(Exception):
    pass


class KB:
    def __init__(self, nc):
        self.nc = nc
        self.cnt = {}
        self.sems = {}
        self.last = {}

    def sem(self, st, name):
        self.sems[name] = st.enter_context(self.nc.semaphore("s_" + name))
        self.cnt[name] = 0
        self.last[name] = {}

    def inc(self, inst, sem, amt=1):
        inst.then_inc(self.sems[sem], amt)
        self.cnt[sem] += amt
        return self.cnt[sem]

    def val(self, sem):
        return self.cnt[sem]

    def wait(self, eng, sem, v=None):
        if v is None:
            v = self.cnt[sem]
        if v <= 0:
            return
        key = str(getattr(eng, "engine", id(eng)))
        if self.last[sem].get(key, 0) >= v:
            return
        eng.wait_ge(self.sems[sem], v)
        self.last[sem][key] = v

    def waitg(self, eng, grp):
        if grp is None:
            return
        self.wait(eng, grp[0], grp[1])


def build_program(T=1024):
    NCH = T // CH
    S = T + 3
    PS = min(512, T // 2)
    NPS = T // PS
    HS = HFF // NCORES
    AC_ = D // NCORES

    nc = bass.Bass("TRN2", target_bir_lowering=False, debug=False)

    def din(name, shape, dt=F32):
        return nc.dram_tensor(name, shape, dt, kind="ExternalInput").ap()

    x0 = din("x0", [S, D])
    w_hbcdt = din("w_hbcdt", [D, HBCW])
    w_gate = din("w_gate", [D, DI])
    w_out = din("w_out", [DI, D])
    conv_w = din("conv_w", [128, 34 * 4])
    conv_b = din("conv_b", [128, 34])
    dt_bias = din("dt_bias", [64, 1])
    a_log = din("a_log", [64, 1])
    d_row = din("d_row", [1, H])
    norm_w = din("norm_w", [128, 32])
    in_gamma = din("in_gamma", [1, D])
    in_beta = din("in_beta", [1, D])
    agg_w = din("agg_w", [D, AC_])
    agg_b = din("agg_b", [1, D])
    d1_w = din("d1_w", [D, HS])
    d1_b = din("d1_b", [1, HS])
    d2_w = din("d2_w", [HS, D])
    d2_b = din("d2_b", [1, D])
    pn_gamma = din("pn_gamma", [1, D])
    pn_beta = din("pn_beta", [1, D])
    bc_w = din("bc_w", [D, AC_])
    bc_b = din("bc_b", [1, D])
    gate_w = din("gate_w", [D, D])
    gate_b = din("gate_b", [1, D])
    out_gamma = din("out_gamma", [1, D])
    out_beta = din("out_beta", [1, D])
    ones_r = din("ones_r", [1, 128], BF16)
    ones2_r = din("ones2_r", [2, 128], BF16)
    nones2_r = din("nones2_r", [2, 128], BF16)
    tri_r = din("tri_r", [128, 128], BF16)
    negI_r = din("negI_r", [128, 128], BF16)
    ident_r = din("ident_r", [128, 128], BF16)
    identf_r = din("identf_r", [128, 128], F32)
    onescol_r = din("onescol_r", [128, 1], BF16)
    sel2_r = din("sel2_r", [64, 2 * (NCORES - 1)])

    out = nc.dram_tensor("out", [T, D], F32, kind="ExternalOutput").ap()

    def scratch(name, shape, dt):
        kind = "ExternalOutput" if name in DEBUG_TAPS else "Internal"
        return nc.dram_tensor(name, shape, dt, kind=kind)

    hbc_dram = scratch("hbc_dram", [34, 128, S], BF16)
    prevL_dram = scratch("prevL_dram", [NCH * NST, DI], BF16)
    part_dram = scratch("part_dram", [T, D], F32)
    st_in = nc.dram_tensor("st_in", [NST, DI + 64], BF16)
    st_out = scratch("st_out", [NCORES * NST, DI + 64], BF16)
    ltok_in = nc.dram_tensor("ltok_in", [1, D], F32)
    ltok_out = nc.dram_tensor("ltok_out", [NCORES, D], F32)
    agg_io = nc.dram_tensor("agg_io", [1, AC_], F32)
    agg_go = nc.dram_tensor("agg_go", [NCORES, AC_], F32)
    d2p_io = nc.dram_tensor("d2p_io", [1, D], F32)
    d2p_go = nc.dram_tensor("d2p_go", [1, D], F32)
    bc_io = nc.dram_tensor("bc_io", [1, AC_], F32)
    bc_go = nc.dram_tensor("bc_go", [NCORES, AC_], F32)
    row_rr = nc.dram_tensor("row_rr", [1, HFF], F32)
    row_rr_bf = nc.dram_tensor("row_rr_bf", [1, HFF], BF16)
    taps = {}
    for t_ in DEBUG_TAPS:
        if t_ == "xn1":
            taps["xn1"] = nc.dram_tensor("tap_xn1", [128, KC * S], BF16, kind="ExternalOutput").ap()
        if t_ == "cum":
            taps["cum"] = nc.dram_tensor("tap_cum", [64, NCH * CH], F32, kind="ExternalOutput").ap()
        if t_ == "yh0":
            taps["yh0"] = nc.dram_tensor("tap_yh0", [128, 32 * T], BF16, kind="ExternalOutput").ap()
        if t_ == "yh":
            taps["yh"] = nc.dram_tensor("tap_yh", [128, 32 * T], BF16, kind="ExternalOutput").ap()
        if t_ == "hidden":
            taps["hidden"] = nc.dram_tensor("tap_hidden", [128, 32 * T], BF16, kind="ExternalOutput").ap()
        if t_ == "x1t":
            taps["x1t"] = nc.dram_tensor("tap_x1t", [128, NCH * D], BF16, kind="ExternalOutput").ap()
        if t_ == "gate":
            taps["gate"] = nc.dram_tensor("tap_gate", [128, NCH * D], BF16, kind="ExternalOutput").ap()
        if t_ == "thought":
            taps["thought"] = nc.dram_tensor("tap_thought", [1, D], F32, kind="ExternalOutput").ap()
        if t_ == "dtf":
            taps["dtf"] = nc.dram_tensor("tap_dtf", [64, S], F32, kind="ExternalOutput").ap()
        if t_ == "whm":
            taps["whm"] = nc.dram_tensor("tap_whm", [64, 4 * (NCORES - 1)], F32, kind="ExternalOutput").ap()
        if t_ == "selv":
            taps["selv"] = nc.dram_tensor("tap_selv", [64, 2 * (NCORES - 1)], F32, kind="ExternalOutput").ap()
        if t_ == "rows7":
            taps["rows7"] = nc.dram_tensor("tap_rows7", [NCORES - 1, 64], F32, kind="ExternalOutput").ap()
        if t_ == "wrep":
            taps["wrep"] = nc.dram_tensor("tap_wrep", [128, (NCORES - 1) * H], F32, kind="ExternalOutput").ap()
        if t_ == "incs":
            taps["incs"] = nc.dram_tensor("tap_incs", [128, DI], BF16, kind="ExternalOutput").ap()

    b = KB(nc)

    _WRAP_OPS = {"tensor_copy", "tensor_tensor", "tensor_scalar", "tensor_add",
                 "tensor_mul", "tensor_sub", "tensor_max", "tensor_relu",
                 "scalar_tensor_tensor", "tensor_tensor_scan", "tensor_reduce",
                 "reduce_sum", "reduce_max", "bn_stats", "bn_aggr", "memset",
                 "reciprocal", "activation", "copy", "add", "mul",
                 "tensor_scalar_add", "tensor_scalar_mul", "tensor_single_scalar"}

    class SafeEng:

        def __init__(self, eng, sem):
            self._eng = eng
            self._sem = sem
            self.engine = getattr(eng, "engine", None)

        def __getattr__(self, name):
            f = getattr(self._eng, name)
            if name in _WRAP_OPS:
                def g(*a, **k):
                    small = False
                    if a and name != "scalar_tensor_tensor":
                        fs = getattr(a[0], "free_size", None)
                        if callable(fs):
                            fs = fs()
                        small = fs is not None and fs < 128
                    if SAFE_EDGES or small:
                        b.wait(self._eng, self._sem)
                    return f(*a, **k)
                return g
            return f

    sy, gp, te = nc.sync, nc.gpsimd, nc.tensor
    ve = SafeEng(nc.vector, "v")
    ac = SafeEng(nc.scalar, "a")

    def V(eng, sem):
        if SAFE_EDGES:
            b.wait(eng, sem)

    with ExitStack() as g0:
        def mk(stk):
            def sb(name, shape, dt, side=None):
                return stk.enter_context(nc.sbuf_tensor(name, shape, dt, side=side))
            return sb

        sb0 = mk(g0)
        ones_sb = sb0("ones_sb", [1, 128], BF16)
        ident_sb = sb0("ident_sb", [128, 128], BF16)
        onescol_sb = sb0("onescol_sb", [128, 1], BF16)
        nwc = sb0("nwc", [128, 32], F32)
        one11 = sb0("one11", [1, 1], BF16)
        rowa = sb0("rowa", [1, D], F32)
        rowbf = sb0("rowbf", [1, D], BF16)
        snm = sb0("snm", [128, 1], F32)
        nm = sb0("nm", [128, 1], F32)
        sq2 = sb0("sq2", [128, 1], F32)
        rsc = sb0("rsc", [128, 1], F32)
        hr_v = sb0("hr_v", [128, 1], F32)
        hr_y = sb0("hr_y", [128, 1], F32)
        hr_t = sb0("hr_t", [128, 1], F32)
        hr_v2 = sb0("hr_v2", [128, 1], F32)
        s11a = sb0("s11a", [1, 1], F32)
        s11b = sb0("s11b", [1, 1], F32)
        s11c = sb0("s11c", [1, 1], F32)
        epsln_c = sb0("epsln_c", [128, 1], F32)
        epsffn_c = sb0("epsffn_c", [128, 1], F32)
        onef_c = sb0("onef_c", [128, 1], F32)
        rcols = sb0("rcols", [128, NCH], F32)
        pk = [g0.enter_context(nc.psum_tensor(f"pk{i}", [128, 512], F32)) for i in range(7)]
        pbf = g0.enter_context(nc.psum_tensor("pbf", [128, 1024], BF16))
        NPOOL = 6
        for name in (["cc", "t", "v", "a", "stg0", "stg1", "stgt", "plst", "ptst", "outst"]
                     + [f"hw{i}" for i in range(NPOOL)]
                     + [f"sw{i}" for i in range(NPOOL)]):
            b.sem(g0, name)
        rr = {"hw": 0, "sw": 0}

        def _dma(queue, pairs, sem=None, slow=False):
            eng = sy if queue == "hw" else gp
            if sem is None:
                sem = f"{queue}{rr[queue] % NPOOL}"
                rr[queue] += 1
                if b.cnt[sem] > 0:
                    b.wait(eng, sem, b.cnt[sem])
            for dst, src_ in pairs:
                if slow:
                    inst = eng.dma_start(out=dst, in_=src_, allow_slow_non_contiguous=True)
                else:
                    inst = eng.dma_start(out=dst, in_=src_)
                b.inc(inst, sem, 16)
            return (sem, b.cnt[sem])

        def fence():
            sems = ["t", "v", "a", "stg0", "stg1", "stgt", "plst", "ptst", "outst"] + \
                   [f"hw{i}" for i in range(NPOOL)] + [f"sw{i}" for i in range(NPOOL)]
            for eng in (sy, gp, ve, ac, te):
                for s_ in sems:
                    b.wait(eng, s_, b.cnt[s_])

        _scope = [None, None]

        def mark(name):
            if _scope[0] is not None:
                nc.leave_named_scope(_scope[0], _scope[1], False)
                _scope[0] = None
            if name:
                sid, _ = nc.enter_named_scope(name, False)
                _scope[0], _scope[1] = name, sid

        def stop(tag):
            if STOP_AFTER == tag:
                fence()
                for eng in (sy, gp, nc.vector, nc.scalar, te):
                    b.wait(eng, "cc", b.cnt["cc"])
                raise _StopBuild(nc)

        def hdma(dst, src, sem=None, slow=False):
            return _dma("hw", [(dst, src)], sem=sem, slow=slow)

        def tap_dma(dst, src):
            grp = _dma("hw", [(dst, src)])
            for e_ in (nc.vector, nc.scalar, nc.tensor, gp):
                b.waitg(e_, grp)

        def gdma(dst, src, sem=None):
            return _dma("sw", [(dst, src)], sem=sem)

        def ln_acc(src_ap, sink_ap, rows=128, width=D, epsc=None, lut=True,
                   eps_f=EPS_LN):
            if epsc is None:
                epsc = epsln_c
            b.wait(ac, "v")
            V(ac, "a")
            b.inc(ac.activation(sink_ap, src_ap, AF.Copy, accum_out=snm[0:rows, :]), "a")
            b.wait(ve, "a")
            V(ve, "v")
            b.inc(ve.tensor_scalar(nm[0:rows, :], snm[0:rows, :], -1.0 / width,
                                   None, ALU.mult), "v")
            b.wait(ac, "v")
            b.inc(ac.activation(sink_ap, src_ap, AF.Square, bias=nm[0:rows, 0:1],
                                accum_out=sq2[0:rows, :]), "a")
            if lut:
                b.inc(ac.activation(rsc[0:rows, :], sq2[0:rows, :], AF.Ln,
                                    scale=1.0 / width, bias=epsc[0:rows, 0:1]), "a")
                b.inc(ac.activation(rsc[0:rows, :], rsc[0:rows, :], AF.Exp, scale=-0.5), "a")
            else:
                b.wait(ve, "a")
                V(ve, "v")
                b.inc(ve.tensor_scalar(hr_v[0:rows, :], sq2[0:rows, :], 1.0 / width,
                                       eps_f, ALU.mult, ALU.add), "v")
                b.inc(ve.reciprocal(hr_y[0:rows, :], hr_v[0:rows, :]), "v")
                b.inc(ve.tensor_copy(hr_t[0:rows, :], hr_y[0:rows, :]), "v")
                for _ in range(10):
                    b.inc(ve.reciprocal(hr_v2[0:rows, :], hr_t[0:rows, :]), "v")
                    b.inc(ve.tensor_tensor(hr_v2[0:rows, :], hr_y[0:rows, :],
                                           hr_v2[0:rows, :], ALU.mult), "v")
                    b.inc(ve.tensor_tensor(hr_t[0:rows, :], hr_t[0:rows, :],
                                           hr_v2[0:rows, :], ALU.add), "v")
                    b.inc(ve.tensor_scalar(hr_t[0:rows, :], hr_t[0:rows, :], 0.5,
                                           None, ALU.mult), "v")
                b.inc(ve.tensor_copy(rsc[0:rows, :], hr_t[0:rows, :]), "v")

        yh_cm = nc.sbuf_tensor("yh", [128, 32 * T], BF16, side="right")
        yh = yh_cm.__enter__()

        with ExitStack() as s1:
            sb1 = mk(s1)
            ones2_sb = sb1("ones2_sb", [2, 128], BF16)
            nones2_sb = sb1("nones2_sb", [2, 128], BF16)
            tri_sb = sb1("tri_sb", [128, 128], BF16)
            negI_sb = sb1("negI_sb", [128, 128], BF16)
            identf_sb = sb1("identf_sb", [128, 128], F32)
            convw_sb = sb1("convw_sb", [128, 34 * 4], F32)
            convb_sb = sb1("convb_sb", [128, 34], F32)
            dtb_sb = sb1("dtb_sb", [64, 1], F32)
            acol = sb1("acol", [64, 1], F32)
            drep = sb1("drep", [128, H], F32)
            selv = sb1("selv", [64, 2 * (NCORES - 1)], F32)
            ones64 = sb1("ones64", [64, CH], F32)
            zcol = sb1("zcol", [64, 1], F32)
            x_n1 = sb1("x_n1", [128, KC * S], BF16)
            c_fm = sb1("c_fm", [128, T], BF16)
            cumS = sb1("cumS", [64, NCH * CH], F32)
            dprep = sb1("dprep", [128, NCH * H], F32)
            dt_fm = sb1("dt_fm", [64, S], F32)
            spl = sb1("spl", [64, S], F32)

            wv_const = _dma("hw", [(ones_sb[:, :], ones_r), (ones2_sb[:, :], ones2_r),
                             (nones2_sb[:, :], nones2_r), (tri_sb[:, :], tri_r),
                             (negI_sb[:, :], negI_r), (ident_sb[:, :], ident_r),
                             (identf_sb[:, :], identf_r), (onescol_sb[:, :], onescol_r),
                             (convw_sb[:, :], conv_w), (convb_sb[:, :], conv_b),
                             (dtb_sb[:, :], dt_bias), (acol[:, :], a_log),
                             (nwc[:, :], norm_w),
                             (selv[:, :], sel2_r)])
            wv_drow = gdma(rowbf[0:1, 0:H], d_row)
            b.waitg(ac, wv_const)
            b.inc(ac.activation(acol[:, :], acol[:, :], AF.Exp), "a")
            b.wait(ve, "a")
            V(ve, "v")
            b.inc(ve.tensor_scalar(acol[:, :], acol[:, :], -1.0, None, ALU.mult), "v")
            b.inc(ve.memset(ones64[:, :], 1.0), "v")
            b.inc(ve.memset(epsln_c[:, :], EPS_LN), "v")
            b.inc(ve.memset(epsffn_c[:, :], EPS_FFN), "v")
            b.inc(ve.memset(onef_c[:, :], 1.0), "v")
            b.inc(ve.memset(zcol[:, :], 0.0), "v")
            b.inc(ve.memset(one11[:, :], 1.0), "v")
            b.waitg(te, wv_drow)
            b.waitg(te, wv_const)
            b.inc(te.matmul(pk[0][:, 0:H], ones_sb[0:1, :], rowbf[0:1, 0:H],
                            start=True, stop=True), "t")
            b.wait(ve, "t")
            b.inc(ve.tensor_copy(drep[:, :], pk[0][:, 0:H]), "v")
            v_pre = b.val("v")

            mark("A")
            with ExitStack() as sa:
                sba = mk(sa)
                x0r = sba("x0r", [128, 2 * D], F32)
                xnt = sba("xnt", [128, D], BF16)
                NSC = (S + 127) // 128
                for sc in range(NSC):
                    rows = min(128, S - sc * 128)
                    xr = x0r[0:rows, (sc % 2) * D:(sc % 2) * D + D]
                    if sc >= 2:
                        b.wait(sy, "v")
                    wv = hdma(xr, x0[sc * 128:sc * 128 + rows, :])
                    b.waitg(ac, wv)
                    b.waitg(ve, wv)
                    ln_acc(xr, xnt[0:rows, :], rows)
                    b.wait(ve, "a")
                    V(ve, "v")
                    b.inc(ve.tensor_scalar(xnt[0:rows, :], xr, nm[0:rows, 0:1], rsc[0:rows, 0:1],
                                           ALU.add, ALU.mult), "v")
                    if "A" in FENCE_EVERY:
                        fence()
                    for half in range(2):
                        b.wait(te, "v")
                        for kk in range(8):
                            kc = half * 8 + kk
                            b.inc(te.transpose(pbf[:, kk * 128:kk * 128 + rows],
                                               xnt[0:rows, kc * 128:(kc + 1) * 128],
                                               ident_sb[0:rows, 0:rows]), "t")
                        b.wait(ve, "t")
                        b.inc(ve.tensor_copy(
                            x_n1[:, :].rearrange("p (kc s) -> p kc s", kc=KC)
                                [:, half * 8:(half + 1) * 8, sc * 128:sc * 128 + rows],
                            pbf[:, :].rearrange("p (kk s) -> p kk s", kk=8)[:, :, 0:rows]), "v")
                        b.wait(te, "v")
            fence()
            v_xn1 = b.val("v")
            if "xn1" in taps:
                b.wait(sy, "v", v_xn1)
                tap_dma(taps["xn1"], x_n1[:, :])
            stop("A")

            mark("B")
            with ExitStack() as sB:
                sbB = mk(sB)
                wring = sbB("wring", [128, 3 * 768], BF16)
                evst = sbB("evst", [128, 2 * PS + 4], BF16)
                tailacc = sbB("tailacc", [128, 35 * 3], F32)
                blocks = [(cb, 128) for cb in range(34)] + [(34, 64)]
                groups = [blocks[i:i + 6] for i in range(0, 35, 6)]
                ptail = pk[6]
                t_ring = {}
                stg_grp = {}
                b.inc(ve.memset(tailacc[:, :], 0.0), "v")
                for pi in range(NPS):
                    s0_, s1_ = pi * PS, (pi + 1) * PS
                    for gi, grp in enumerate(groups):
                        c0 = grp[0][0] * 128
                        cw = sum(w for _, w in grp)
                        b.wait(te, "v")
                        b.wait(te, "a")
                        if "B" in FENCE_EVERY:
                            fence()
                        for kc in range(KC):
                            ring = (gi * KC + kc) % 3
                            wt = wring[:, ring * 768:ring * 768 + cw]
                            if t_ring.get(ring) is not None:
                                b.wait(gp, "t", t_ring[ring])
                            wv = gdma(wt, w_hbcdt[kc * 128:(kc + 1) * 128, c0:c0 + cw])
                            b.waitg(te, wv)
                            b.wait(te, "v", v_xn1)
                            off = 0
                            mm = None
                            if pi == 0:
                                b.wait(te, "v")
                            for bi, (cb, wdt) in enumerate(grp):
                                mm = te.matmul(pk[bi][0:wdt, 0:PS], wt[:, off:off + wdt],
                                               x_n1[:, kc * S + s0_:kc * S + s1_],
                                               start=(kc == 0), stop=(kc == KC - 1))
                                if pi == 0:
                                    mm = te.matmul(
                                        ptail[0:wdt, bi * 3:bi * 3 + 3],
                                        wt[:, off:off + wdt],
                                        x_n1[:, kc * S + T:kc * S + T + 3],
                                        start=True, stop=True)
                                off += wdt
                            b.inc(mm, "t")
                            t_ring[ring] = b.val("t")
                            if pi == 0:
                                ncols = 3 * len(grp)
                                b.wait(ve, "t")
                                b.inc(ve.tensor_tensor(
                                    tailacc[:, grp[0][0] * 3:grp[0][0] * 3 + ncols],
                                    tailacc[:, grp[0][0] * 3:grp[0][0] * 3 + ncols],
                                    ptail[:, 0:ncols], ALU.add), "v")
                        tv = b.val("t")
                        for bi, (cb, wdt) in enumerate(grp):
                            if cb == 34:
                                b.wait(ve, "t", tv)
                                V(ve, "v")
                                b.inc(ve.tensor_copy(dt_fm[0:64, s0_:s1_], pk[bi][0:64, 0:PS]), "v")
                                if pi == NPS - 1:
                                    b.inc(ve.tensor_copy(
                                        dt_fm[0:64, T:T + 3],
                                        tailacc[0:64, 34 * 3:34 * 3 + 3]), "v")
                                continue
                            stg = evst[0:wdt, (bi % 2) * PS:(bi % 2) * PS + PS]
                            if bi % 2 == 0:
                                b.wait(ac, "t", tv)
                                b.waitg(ac, stg_grp.get(bi % 2))
                                V(ac, "a")
                                b.inc(ac.copy(stg, pk[bi][0:wdt, 0:PS]), "a")
                                b.wait(sy, "a")
                            else:
                                b.wait(ve, "t", tv)
                                b.waitg(ve, stg_grp.get(bi % 2))
                                V(ve, "v")
                                b.inc(ve.tensor_copy(stg, pk[bi][0:wdt, 0:PS]), "v")
                                b.wait(sy, "v")
                            stg_grp[bi % 2] = hdma(hbc_dram.ap()[cb, 0:wdt, s0_:s1_], stg,
                                                   sem="stg%d" % (bi % 2))
                        if pi == NPS - 1:
                            for bi, (cb, wdt) in enumerate(grp):
                                if cb == 34:
                                    continue
                                b.wait(ve, "t", tv)
                                b.waitg(ve, stg_grp.get("tail"))
                                V(ve, "v")
                                tst = evst[0:wdt, 2 * PS:2 * PS + 3]
                                b.inc(ve.tensor_copy(
                                    tst, tailacc[0:wdt, cb * 3:cb * 3 + 3]), "v")
                                b.wait(sy, "v")
                                stg_grp["tail"] = hdma(hbc_dram.ap()[cb, 0:wdt, T:T + 3], tst,
                                                       sem="stgt")
            hw_hbc = [("stg0", b.cnt["stg0"]), ("stg1", b.cnt["stg1"]),
                      ("stgt", b.cnt["stgt"])]
            fence()
            stop("B")

            mark("C0")
            b.wait(ve, "v")
            V(ve, "v")
            b.inc(ve.tensor_scalar(dt_fm[:, 0:S], dt_fm[:, 0:S], dtb_sb[:, 0:1], None, ALU.add), "v")
            b.wait(ac, "v")
            V(ac, "a")
            b.inc(ac.activation(spl[:, 0:S], dt_fm[:, 0:S], AF.Abs), "a")
            b.inc(ac.activation(spl[:, 0:S], spl[:, 0:S], AF.Exp, scale=-1.0), "a")
            b.inc(ac.activation(spl[:, 0:S], spl[:, 0:S], AF.Ln, bias=onef_c[0:64, 0:1]), "a")
            b.inc(ac.activation(dt_fm[:, 0:S], dt_fm[:, 0:S], AF.Relu), "a")
            b.wait(ve, "a")
            b.inc(ve.tensor_tensor(dt_fm[:, 0:S], dt_fm[:, 0:S], spl[:, 0:S], ALU.add), "v")
            b.inc(ve.tensor_scalar(spl[:, 0:S], dt_fm[:, 0:S], acol[:, 0:1], None, ALU.mult), "v")
            v_dt = b.val("v")
            if "dtf" in taps:
                b.wait(sy, "v", v_dt)
                tap_dma(taps["dtf"], dt_fm[:, :])

            mark("CD0")
            with ExitStack() as sc_:
                sbc = mk(sc_)
                s0m = sbc("s0m", [128, NCH * CH], BF16)
                b_tm = sbc("b_tm", [128, NCH * NST], BF16)
                dt_tm = sbc("dt_tm", [128, NCH * H], F32)
                ncum_tm = sbc("ncum_tm", [128, NCH * H], F32)
                s_tm = sbc("s_tm", [128, NCH * H], F32)
                cdrep = sbc("cdrep", [128, NCH * H], F32)
                prev = sbc("prev", [128, DI], BF16)
                cdrow = sbc("cdrow", [1, NCH * H], F32)
                dprow = sbc("dprow", [1, (NCH + 1) * H], F32)
                hbcr = sbc("hbcr", [128, 34 * (CH + 3)], BF16)
                xfr = sbc("xfr", [128, 32 * CH], BF16)
                bfr = sbc("bfr", [128, CH], BF16)
                xtmr = sbc("xtmr", [128, 32 * CH], BF16)
                xs2 = sbc("xs2", [128, 8 * 64], BF16)
                xd2 = sbc("xd2", [128, 8 * 64], BF16)
                er = sbc("er", [128, 512], BF16)
                e2r = sbc("e2r", [128, 512], BF16)
                wtr = sbc("wtr", [128, 8 * 128], BF16)
                hilo = sbc("hilo", [2, CH * H], BF16)
                cumf = sbc("cumf", [64, CH], F32)
                hfl = sbc("hfl", [64, CH], BF16)
                lfl = sbc("lfl", [64, CH], BF16)
                sm64a = sbc("sm64a", [64, CH], F32)
                sm64b = sbc("sm64b", [64, CH], F32)
                convs = sbc("convs", [128, 11 * CH], F32)
                rowh = sbc("rowh", [1, H], F32)

                b.wait(ve, "v", v_pre)
                V(ve, "v")
                b.inc(ve.memset(prev[:, :], 0.0), "v")
                b.inc(ve.memset(dprow[0:1, 0:H], 1.0), "v")

                for c in range(NCH):
                    if "C" in FENCE_EVERY:
                        fence()
                    tok0 = c * CH
                    o_h = c * H
                    b.wait(ve, "v", v_dt)
                    b.wait(sy, "v")
                    b.wait(sy, "t")
                    V(ve, "v")
                    b.inc(ve.tensor_tensor_scan(cumf[:, :], ones64[:, :],
                                                spl[:, 3 + tok0:3 + tok0 + CH], zcol[:, 0:1],
                                                ALU.mult, ALU.add), "v")
                    b.inc(ve.tensor_copy(cumS[:, tok0:tok0 + CH], cumf[:, :]), "v")
                    b.inc(ve.tensor_copy(hfl[:, :], cumf[:, :]), "v")
                    b.inc(ve.tensor_tensor(lfl[:, :], cumf[:, :], hfl[:, :], ALU.subtract), "v")
                    vv = b.val("v")
                    b.wait(sy, "v", vv)
                    wv_hilo = _dma("hw", [(hilo[0:1, :], hfl[:, :]),
                                          (hilo[1:2, :], lfl[:, :])])
                    b.wait(sy, "v", vv)
                    b.wait(te, "v", vv)
                    b.inc(te.transpose(pk[5][:, 0:64], cumf[:, :], identf_sb[0:64, 0:64]), "t")
                    b.wait(ve, "t")
                    b.inc(ve.tensor_scalar(ncum_tm[:, o_h:o_h + H], pk[5][:, 0:64], -1.0,
                                           None, ALU.mult), "v")
                    b.wait(te, "v")
                    b.inc(te.transpose(pk[5][:, 64:128], dt_fm[:, 3 + tok0:3 + tok0 + CH],
                                       identf_sb[0:64, 0:64]), "t")
                    b.wait(ve, "t")
                    b.inc(ve.tensor_copy(dt_tm[:, o_h:o_h + H], pk[5][:, 64:128]), "v")
                    b.wait(ac, "v", vv)
                    V(ac, "a")
                    b.inc(ac.activation(sm64a[:, :], cumf[:, :], AF.Exp, scale=-1.0,
                                        bias=cumf[:, CH - 1:CH]), "a")
                    b.wait(ve, "a")
                    b.inc(ve.tensor_tensor(sm64b[:, :], sm64a[:, :],
                                           dt_fm[:, 3 + tok0:3 + tok0 + CH], ALU.mult), "v")
                    b.wait(te, "v")
                    b.inc(te.transpose(pk[5][:, 0:64], sm64b[:, :], identf_sb[0:64, 0:64]), "t")
                    b.wait(ve, "t")
                    b.inc(ve.tensor_copy(s_tm[:, o_h:o_h + H], pk[5][:, 0:64]), "v")
                    wv_cd = hdma(rowh[0:1, :], cumf[:, CH - 1:CH])
                    b.waitg(ac, wv_cd)
                    b.inc(ac.activation(cdrow[0:1, o_h:o_h + H], rowh[0:1, :], AF.Exp), "a")
                    b.wait(ve, "a")
                    if c > 0:
                        b.inc(ve.tensor_tensor(dprow[0:1, o_h:o_h + H],
                                               dprow[0:1, o_h - H:o_h],
                                               cdrow[0:1, o_h - H:o_h], ALU.mult), "v")
                    b.inc(ve.tensor_copy(rowbf[0:1, 0:H], cdrow[0:1, o_h:o_h + H]), "v")
                    b.inc(ve.tensor_copy(rowbf[0:1, H:2 * H], dprow[0:1, o_h:o_h + H]), "v")
                    b.wait(te, "v")
                    b.inc(te.matmul(pk[5][:, 0:64], ones_sb[0:1, :], rowbf[0:1, 0:H],
                                    start=True, stop=True), "t")
                    b.inc(te.matmul(pk[5][:, 64:128], ones_sb[0:1, :], rowbf[0:1, H:2 * H],
                                    start=True, stop=True), "t")
                    b.wait(ve, "t")
                    b.inc(ve.tensor_copy(cdrep[:, o_h:o_h + H], pk[5][:, 0:64]), "v")
                    b.inc(ve.tensor_copy(dprep[:, o_h:o_h + H], pk[5][:, 64:128]), "v")
                    b.wait(te, "v")

                    b.wait(sy, "a")
                    b.wait(sy, "v")
                    for g_ in hw_hbc:
                        b.waitg(sy, g_)
                    wv = hdma(hbcr[:, :].rearrange("p (cb s) -> p cb s", cb=34),
                              hbc_dram.ap()[:, :, tok0:tok0 + CH + 3].rearrange("cb p s -> p cb s"))
                    b.waitg(ve, wv)
                    b.waitg(ac, wv)
                    W_ = CH + 3
                    av_prev = b.val("a")
                    for cb0, ncb in [(0, 11), (11, 11), (22, 10), (32, 1), (33, 1)]:
                        b.wait(ve, "a", av_prev)
                        V(ve, "v")
                        for cb in range(cb0, cb0 + ncb):
                            seg = hbcr[:, cb * W_:(cb + 1) * W_]
                            cv = convs[:, (cb - cb0) * CH:(cb - cb0 + 1) * CH]
                            b.inc(ve.tensor_scalar(cv, seg[:, 0:CH],
                                                   convw_sb[:, cb * 4:cb * 4 + 1],
                                                   convb_sb[:, cb:cb + 1],
                                                   ALU.mult, ALU.add), "v")
                            for k in range(1, 4):
                                b.inc(ve.scalar_tensor_tensor(cv, seg[:, k:k + CH],
                                                              convw_sb[:, cb * 4 + k:cb * 4 + k + 1],
                                                              cv, ALU.mult, ALU.add), "v")
                        if cb0 < 32:
                            dst = xfr[:, cb0 * CH:(cb0 + ncb) * CH]
                        elif cb0 == 32:
                            dst = bfr[:, :]
                        else:
                            dst = c_fm[:, tok0:tok0 + CH]
                        b.wait(ac, "v")
                        V(ac, "a")
                        b.inc(ac.activation(dst, convs[:, 0:ncb * CH], AF.Silu), "a")
                        av_prev = b.val("a")
                    av_conv = b.val("a")

                    for half in range(4):
                        b.wait(te, "a", av_conv)
                        b.wait(te, "v")
                        for kk in range(8):
                            cb = half * 8 + kk
                            b.inc(te.transpose(pbf[:, kk * 128:(kk + 1) * 128],
                                               xfr[:, cb * CH:(cb + 1) * CH], ident_sb[:, :]), "t")
                        b.wait(ve, "t")
                        V(ve, "v")
                        b.inc(ve.tensor_copy(xtmr[:, half * 1024:(half + 1) * 1024], pbf[:, :]), "v")
                    b.wait(te, "v")
                    b.inc(te.transpose(pbf[:, 0:128], bfr[:, :], ident_sb[:, :]), "t")
                    b.wait(ve, "t")
                    b.inc(ve.tensor_copy(b_tm[:, c * NST:(c + 1) * NST], pbf[:, 0:128]), "v")
                    b.wait(te, "a", av_conv)
                    b.wait(te, "v")
                    b.inc(te.matmul(pk[4][:, 0:128], bfr[:, :], c_fm[:, tok0:tok0 + CH],
                                    start=True, stop=True), "t")
                    b.wait(ve, "t")
                    b.inc(ve.tensor_copy(s0m[:, tok0:tok0 + CH], pk[4][:, 0:128]), "v")
                    b.wait(te, "v")
                    v_chunkprep = b.val("v")

                    b.wait(sy, "v")
                    wv_plexp = hdma(prevL_dram.ap()[c * NST:(c + 1) * NST, :], prev[:, :],
                                    sem="plst")
                    carry_waited = [False]

                    for g8 in range(8):
                        if "G8" in FENCE_EVERY:
                            fence()
                        for q4 in range(2):
                            pR = pk[q4]
                            b.waitg(te, wv_hilo)
                            b.wait(te, "a")
                            mm = None
                            for i_ in range(4):
                                h = g8 * 8 + q4 * 4 + i_
                                sl = pR[:, i_ * 128:(i_ + 1) * 128]
                                te.matmul(sl, ones2_sb[:, :], hilo[:, h * CH:(h + 1) * CH],
                                          start=True, stop=False)
                                te.matmul(sl, hilo[:, h * CH:(h + 1) * CH], nones2_sb[:, :],
                                          start=False, stop=False)
                                mm = te.matmul(sl, tri_sb[:, :], negI_sb[:, :],
                                               start=False, stop=True)
                            b.inc(mm, "t")
                            b.wait(ac, "t")
                            b.wait(ac, "v")
                            V(ac, "a")
                            b.inc(ac.activation((er if q4 == 0 else e2r)[:, :], pR[:, :], AF.Exp), "a")
                        av8 = b.val("a")
                        pY = pk[2]
                        pS = pk[3]
                        tv_prevgrp = b.val("t")
                        b.wait(ve, "a", av8)
                        b.wait(ve, "t", tv_prevgrp)
                        V(ve, "v")
                        for i in range(8):
                            h = g8 * 8 + i
                            esl = (er if i < 4 else e2r)[:, (i % 4) * 128:(i % 4 + 1) * 128]
                            b.inc(ve.scalar_tensor_tensor(wtr[:, i * 128:(i + 1) * 128], esl,
                                                          dt_tm[:, o_h + h:o_h + h + 1],
                                                          s0m[:, tok0:tok0 + CH],
                                                          ALU.mult, ALU.mult), "v")
                            b.inc(ve.tensor_scalar(xs2[:, i * 64:(i + 1) * 64],
                                                   xtmr[:, h * 64:(h + 1) * 64],
                                                   s_tm[:, o_h + h:o_h + h + 1], None, ALU.mult), "v")
                            b.inc(ve.tensor_scalar(xd2[:, i * 64:(i + 1) * 64],
                                                   xtmr[:, h * 64:(h + 1) * 64],
                                                   drep[:, h:h + 1], None, ALU.mult), "v")
                        vv = b.val("v")
                        b.wait(te, "v", vv)
                        for i in range(8):
                            h = g8 * 8 + i
                            v_ = i % 2
                            q_ = i // 2
                            ysl = pY[64 * v_:64 * v_ + 64, q_ * 128:(q_ + 1) * 128]
                            te.matmul(ysl, xtmr[:, h * 64:(h + 1) * 64],
                                      wtr[:, i * 128:(i + 1) * 128],
                                      start=True, stop=False)
                            b.inc(te.matmul(ysl, xd2[:, i * 64:(i + 1) * 64],
                                            ident_sb[:, :], start=False,
                                            stop=True), "t")
                            b.inc(te.matmul(pS[:, i * 64:(i + 1) * 64],
                                            b_tm[:, c * NST:(c + 1) * NST],
                                            xs2[:, i * 64:(i + 1) * 64],
                                            start=True, stop=True), "t")
                        tv8 = b.val("t")
                        b.wait(ve, "t", tv8)
                        V(ve, "v")
                        b.inc(ve.tensor_copy(
                            yh[:, :].rearrange("p (cb t) -> p cb t", cb=32)
                                [:, g8 * 4:(g8 + 1) * 4, tok0:tok0 + CH],
                            pY[:, :].rearrange("p (q i) -> p q i", q=4)), "v")
                        if not carry_waited[0]:
                            b.waitg(ve, wv_plexp)
                            carry_waited[0] = True
                        for i in range(8):
                            h = g8 * 8 + i
                            b.inc(ve.scalar_tensor_tensor(prev[:, h * 64:(h + 1) * 64],
                                                          prev[:, h * 64:(h + 1) * 64],
                                                          cdrep[:, o_h + h:o_h + h + 1],
                                                          pS[:, i * 64:(i + 1) * 64],
                                                          ALU.mult, ALU.add), "v")
                        b.wait(te, "v")

                b.inc(ve.tensor_tensor(dprow[0:1, NCH * H:(NCH + 1) * H],
                                       dprow[0:1, (NCH - 1) * H:NCH * H],
                                       cdrow[0:1, (NCH - 1) * H:NCH * H], ALU.mult), "v")
                b.inc(ve.tensor_copy(rowbf[0:1, 0:H], dprow[0:1, NCH * H:(NCH + 1) * H]), "v")
                b.inc(ve.memset(er[:, 0:H], 0.0), "v")
                b.wait(sy, "v")
                wv_pad = hdma(st_in.ap()[:, DI:DI + 64], er[:, 0:H])
                b.waitg(sy, wv_pad)
                wv_sti = _dma("hw", [(st_in.ap()[:, 0:DI], prev[:, :]),
                                     (st_in.ap()[0:1, DI:DI + 64], rowbf[0:1, 0:H])])
                b.waitg(gp, wv_sti)
                b.inc(gp.collective_compute("AllGather", ALU.bypass,
                                            replica_groups=[list(range(NCORES))],
                                            ins=[st_in.ap().opt()], outs=[st_out.ap().opt()]),
                      "cc")
            fence()
            if "cum" in taps:
                b.wait(sy, "v")
                tap_dma(taps["cum"], cumS[:, :])
            if "yh0" in taps:
                b.wait(sy, "v")
                tap_dma(taps["yh0"], yh[:, :])
            stop("D0")

            mark("D2")
            with ExitStack() as sd:
                sbd = mk(sd)
                inc_sb = sbd("inc_sb", [128, DI], BF16)
                sjr = sbd("sjr", [128, DI], BF16)
                ptr_ = sbd("ptr_", [128, 2 * DI], BF16)
                plr = sbd("plr", [128, 2 * DI], BF16)
                er2 = sbd("er2", [128, 512], BF16)
                e2r2 = sbd("e2r2", [128, 512], BF16)
                chat = sbd("chat", [128, 8 * 128], BF16)
                hilo2 = sbd("hilo2", [2, CH * H], BF16)
                hfl2 = sbd("hfl2", [64, CH], BF16)
                lfl2 = sbd("lfl2", [64, CH], BF16)
                whm = sbd("whm", [64, 4 * (NCORES - 1)], F32)
                rows7 = sbd("rows7", [NCORES - 1, 64], F32)
                ptj = sbd("ptj", [NCORES - 1, 64], BF16)
                wrflat = sbd("wrflat", [1, (NCORES - 1) * H], BF16)
                wrep = sbd("wrep", [128, (NCORES - 1) * H], F32)

                b.wait(sy, "cc")
                b.wait(gp, "cc")
                NJ = NCORES - 1
                deff = whm[:, 0:NJ]
                drev = whm[:, NJ:2 * NJ]
                prodv = whm[:, 2 * NJ:3 * NJ]
                pj = whm[:, 3 * NJ:4 * NJ]
                b.wait(gp, "cc")
                wv = _dma("sw", [(rows7[j:j + 1, :],
                                  st_out.ap()[j * NST:j * NST + 1, DI:DI + 64])
                                 for j in range(NJ)])
                b.waitg(te, wv)
                b.wait(te, "v")
                b.inc(te.transpose(pk[5][0:64, 0:NJ], rows7[0:NJ, :],
                                   identf_sb[0:NJ, 0:NJ]), "t")
                b.wait(ve, "t")
                V(ve, "v")
                b.inc(ve.tensor_copy(deff, pk[5][0:64, 0:NJ]), "v")
                b.inc(ve.tensor_scalar(deff, deff, -1.0, None, ALU.add), "v")
                b.inc(ve.tensor_tensor(deff, deff, selv[:, NJ:2 * NJ], ALU.mult), "v")
                b.inc(ve.tensor_scalar(deff, deff, 1.0, None, ALU.add), "v")
                for j in range(NJ):
                    b.inc(ve.tensor_copy(drev[:, j:j + 1], deff[:, NJ - 1 - j:NJ - j]), "v")
                b.inc(ve.memset(pj[:, :], 1.0), "v")
                b.inc(ve.tensor_tensor_scan(prodv, drev, pj[:, 0:NJ],
                                            pj[:, 0:1], ALU.mult, ALU.mult), "v")
                for j in range(NJ - 1):
                    b.inc(ve.tensor_copy(pj[:, j:j + 1], prodv[:, NJ - 2 - j:NJ - 1 - j]), "v")
                b.inc(ve.memset(pj[:, NJ - 1:NJ], 1.0), "v")
                b.inc(ve.tensor_tensor(pj, pj, selv[:, 0:NJ], ALU.mult), "v")
                b.inc(ve.memset(inc_sb[:, :], 0.0), "v")
                b.wait(te, "v")
                b.inc(te.transpose(pk[5][0:NJ, 64:128], pj[0:64, 0:NJ],
                                   identf_sb[0:64, 0:64]), "t")
                b.wait(ve, "t")
                b.inc(ve.tensor_copy(ptj[0:NJ, :], pk[5][0:NJ, 64:128]), "v")
                b.wait(sy, "v")
                wv_wr = _dma("hw", [(wrflat[0:1, j * H:(j + 1) * H], ptj[j:j + 1, :])
                                    for j in range(NJ)])
                b.waitg(te, wv_wr)
                for j in range(NJ):
                    b.inc(te.matmul(pk[5][:, 128 + (j % 2) * 64:192 + (j % 2) * 64],
                                    ones_sb[0:1, :], wrflat[0:1, j * H:(j + 1) * H],
                                    start=True, stop=True), "t")
                    b.wait(ve, "t")
                    b.inc(ve.tensor_copy(wrep[:, j * H:(j + 1) * H],
                                         pk[5][:, 128 + (j % 2) * 64:192 + (j % 2) * 64]), "v")
                    b.wait(te, "v")
                for j in range(NCORES - 1):
                    b.wait(sy, "v")
                    wv = hdma(sjr[:, :], st_out.ap()[j * NST:(j + 1) * NST, 0:DI])
                    b.waitg(ve, wv)
                    for h in range(H):
                        b.inc(ve.scalar_tensor_tensor(inc_sb[:, h * 64:(h + 1) * 64],
                                                      sjr[:, h * 64:(h + 1) * 64],
                                                      wrep[:, j * H + h:j * H + h + 1],
                                                      inc_sb[:, h * 64:(h + 1) * 64],
                                                      ALU.mult, ALU.add), "v")
                v_inc = b.val("v")
                if "selv" in taps:
                    b.wait(sy, "v", v_inc)
                    tap_dma(taps["selv"], selv[:, :])
                if "rows7" in taps:
                    b.wait(sy, "v", v_inc)
                    tap_dma(taps["rows7"], rows7[:, :])
                if "whm" in taps:
                    b.wait(sy, "v", v_inc)
                    tap_dma(taps["whm"], whm[:, :])
                if "wrep" in taps:
                    b.wait(sy, "v", v_inc)
                    tap_dma(taps["wrep"], wrep[:, :])
                if "incs" in taps:
                    b.wait(sy, "v", v_inc)
                    tap_dma(taps["incs"], inc_sb[:, :])

                for c in range(NCH):
                    if "D2" in FENCE_EVERY:
                        fence()
                    tok0 = c * CH
                    o_h = c * H
                    ring = c % 2
                    if c >= 2:
                        b.wait(sy, "v")
                    b.waitg(sy, ("plst", b.cnt["plst"]))
                    wv = hdma(plr[:, ring * DI:(ring + 1) * DI],
                              prevL_dram.ap()[c * NST:(c + 1) * NST, :])
                    b.wait(sy, "t")
                    V(ve, "v")
                    b.inc(ve.tensor_copy(hfl2[:, :], cumS[:, tok0:tok0 + CH]), "v")
                    b.inc(ve.tensor_tensor(lfl2[:, :], cumS[:, tok0:tok0 + CH],
                                           hfl2[:, :], ALU.subtract), "v")
                    vv = b.val("v")
                    b.wait(sy, "v", vv)
                    wv_h2 = _dma("hw", [(hilo2[0:1, :], hfl2[:, :]),
                                        (hilo2[1:2, :], lfl2[:, :])])
                    b.waitg(ve, wv)
                    b.wait(ve, "v", v_inc)
                    for h in range(H):
                        b.inc(ve.scalar_tensor_tensor(
                            ptr_[:, ring * DI + h * 64:ring * DI + (h + 1) * 64],
                            inc_sb[:, h * 64:(h + 1) * 64],
                            dprep[:, o_h + h:o_h + h + 1],
                            plr[:, ring * DI + h * 64:ring * DI + (h + 1) * 64],
                            ALU.mult, ALU.add), "v")
                    vp = b.val("v")
                    tv8_last = b.val("t")
                    for g8 in range(8):
                        for q4 in range(2):
                            pR = pk[q4]
                            b.waitg(te, wv_h2)
                            b.wait(te, "a")
                            mm = None
                            for i_ in range(4):
                                h = g8 * 8 + q4 * 4 + i_
                                mm = te.matmul(pR[:, i_ * 128:(i_ + 1) * 128], ones2_sb[:, :],
                                               hilo2[:, h * CH:(h + 1) * CH],
                                               start=True, stop=True)
                            b.inc(mm, "t")
                            b.wait(ac, "t")
                            b.wait(ac, "v")
                            V(ac, "a")
                            b.inc(ac.activation((er2 if q4 == 0 else e2r2)[:, :], pR[:, :],
                                                AF.Exp), "a")
                        av8 = b.val("a")
                        pY = pk[2]
                        for i in range(8):
                            h = g8 * 8 + i
                            esl = (er2 if i < 4 else e2r2)[:, (i % 4) * 128:(i % 4 + 1) * 128]
                            csl = chat[:, (i % 2) * 128:(i % 2 + 1) * 128]
                            b.wait(ve, "a", av8)
                            b.wait(ve, "v", vp)
                            if i >= 2:
                                b.wait(ve, "t")
                            V(ve, "v")
                            b.inc(ve.tensor_tensor(csl, c_fm[:, tok0:tok0 + CH], esl,
                                                   ALU.mult), "v")
                            b.wait(te, "v")
                            v_ = i % 2
                            q_ = i // 2
                            b.inc(te.matmul(pY[64 * v_:64 * v_ + 64, q_ * 128:(q_ + 1) * 128],
                                            ptr_[:, ring * DI + h * 64:ring * DI + (h + 1) * 64],
                                            csl, start=True, stop=True), "t")
                        tv8 = b.val("t")
                        b.wait(ve, "t", tv8)
                        ysl = yh[:, :].rearrange("p (cb t) -> p cb t", cb=32)[
                            :, g8 * 4:(g8 + 1) * 4, tok0:tok0 + CH]
                        V(ve, "v")
                        b.inc(ve.tensor_tensor(ysl, ysl,
                                               pY[:, :].rearrange("p (q i) -> p q i", q=4),
                                               ALU.add), "v")
                        b.wait(te, "v")
            fence()
            v_yh = b.val("v")
            if "yh" in taps:
                b.wait(sy, "v", v_yh)
                tap_dma(taps["yh"], yh[:, :])
            stop("D2")

            mark("E")
            with ExitStack() as se:
                sbe = mk(se)
                wring = sbe("wringE", [128, 3 * 768], BF16)
                sgr = sbe("sgr", [128, PS], BF16)
                sq_s = sbe("sq_s", [128, T], BF16)
                rrow = sbe("rrow", [1, T], F32)
                ggroups = [list(range(i, min(i + 6, 32))) for i in range(0, 32, 6)]
                t_ring = {}
                for pi in range(NPS):
                    s0_, s1_ = pi * PS, (pi + 1) * PS
                    for gi, grp in enumerate(ggroups):
                        c0 = grp[0] * 128
                        cw = len(grp) * 128
                        b.wait(te, "v")
                        b.wait(te, "a")
                        if "E" in FENCE_EVERY:
                            fence()
                        for kc in range(KC):
                            ring = (gi * KC + kc) % 3
                            wt = wring[:, ring * 768:ring * 768 + cw]
                            if t_ring.get(ring) is not None:
                                b.wait(gp, "t", t_ring[ring])
                            wv = gdma(wt, w_gate[kc * 128:(kc + 1) * 128, c0:c0 + cw])
                            b.waitg(te, wv)
                            mm = None
                            for bi, cb in enumerate(grp):
                                mm = te.matmul(pk[bi][:, 0:PS], wt[:, bi * 128:(bi + 1) * 128],
                                               x_n1[:, kc * S + 3 + s0_:kc * S + 3 + s1_],
                                               start=(kc == 0), stop=(kc == KC - 1))
                            b.inc(mm, "t")
                            t_ring[ring] = b.val("t")
                        tv = b.val("t")
                        for bi, cb in enumerate(grp):
                            b.wait(ac, "t", tv)
                            b.wait(ac, "v")
                            V(ac, "a")
                            b.inc(ac.activation(sgr[:, 0:PS], pk[bi][:, 0:PS], AF.Silu), "a")
                            b.wait(ve, "a")
                            b.wait(ve, "v", v_yh)
                            ysl = yh[:, cb * T + s0_:cb * T + s1_]
                            V(ve, "v")
                            b.inc(ve.tensor_tensor(ysl, ysl, sgr[:, 0:PS], ALU.mult), "v")
                        b.wait(te, "v")
                v_hid = b.val("v")
                if "hidden" in taps:
                    b.wait(sy, "v", v_hid)
                    tap_dma(taps["hidden"], yh[:, :])
                for cb in range(32):
                    b.wait(ac, "v", v_hid)
                    b.wait(ac, "t")
                    V(ac, "a")
                    b.inc(ac.activation(sq_s[:, 0:T], yh[:, cb * T:(cb + 1) * T], AF.Square), "a")
                    b.wait(te, "a")
                    b.inc(te.matmul(pk[5][0:1, 0:min(512, T)], onescol_sb[:, 0:1],
                                    sq_s[:, 0:min(512, T)], start=(cb == 0), stop=(cb == 31)), "t")
                    if T > 512:
                        b.inc(te.matmul(pk[6][0:1, 0:512], onescol_sb[:, 0:1], sq_s[:, 512:1024],
                                        start=(cb == 0), stop=(cb == 31)), "t")
                b.wait(ac, "t")
                b.inc(ac.activation(rrow[0:1, 0:min(512, T)], pk[5][0:1, 0:min(512, T)], AF.Ln,
                                    scale=1.0 / DI, bias=epsln_c[0:1, 0:1]), "a")
                if T > 512:
                    b.inc(ac.activation(rrow[0:1, 512:1024], pk[6][0:1, 0:512], AF.Ln,
                                        scale=1.0 / DI, bias=epsln_c[0:1, 0:1]), "a")
                b.inc(ac.activation(rrow[0:1, 0:T], rrow[0:1, 0:T], AF.Exp, scale=-0.5), "a")
                b.wait(sy, "a")
                wv_rr = hdma(row_rr.ap()[0:1, 0:T], rrow[0:1, 0:T])
                b.waitg(sy, wv_rr)
                r_hw = hdma(rcols[:, 0:NCH],
                            row_rr.ap()[0:1, 0:T].rearrange("o (tc p) -> (o p) tc", p=128),
                            slow=True)
        fence()
        stop("E")

        with ExitStack() as s2:
            sb2 = mk(s2)
            x1 = sb2("x1", [128, NCH * D], BF16)

            mark("F")
            with ExitStack() as sf:
                sbf = mk(sf)
                wo_sb = sbf("wo_sb", [128, 16 * D], BF16)
                x0r = sbf("x0rF", [128, D], F32)
                pr_r = sbf("pr_r", [128, D], F32)
                outst = sbf("outstF", [128, D], F32)
                for p_ in range(2):
                    wvs = []
                    for kb in range(16):
                        kbg = p_ * 16 + kb
                        if p_ == 1:
                            b.wait(gp, "t")
                        wv = gdma(wo_sb[:, kb * D:(kb + 1) * D], w_out[kbg * 128:(kbg + 1) * 128, :])
                        b.waitg(ve, wv)
                        V(ve, "v")
                        b.inc(ve.tensor_scalar(wo_sb[:, kb * D:(kb + 1) * D],
                                               wo_sb[:, kb * D:(kb + 1) * D],
                                               nwc[:, kbg:kbg + 1], None, ALU.mult), "v")
                    v_w = b.val("v")
                    for tc in range(NCH):
                        if "F" in FENCE_EVERY:
                            fence()
                        tok0 = tc * CH
                        b.wait(te, "v", v_w)
                        mm = None
                        for kb in range(16):
                            for q in range(4):
                                mm = te.matmul(pk[q][:, 0:512],
                                               yh[:, (p_ * 16 + kb) * T + tok0:(p_ * 16 + kb) * T + tok0 + CH],
                                               wo_sb[:, kb * D + q * 512:kb * D + (q + 1) * 512],
                                               start=(kb == 0), stop=(kb == 15))
                        b.inc(mm, "t")
                        tv = b.val("t")
                        if p_ == 0:
                            b.wait(ve, "t", tv)
                            b.waitg(ve, ("ptst", b.cnt["ptst"]))
                            V(ve, "v")
                            for q in range(4):
                                b.inc(ve.tensor_copy(outst[:, q * 512:(q + 1) * 512],
                                                     pk[q][:, 0:512]), "v")
                            b.wait(sy, "v")
                            hdma(part_dram.ap()[tok0:tok0 + CH, :], outst[:, :], sem="ptst")
                            b.wait(te, "v")
                        else:
                            b.wait(sy, "v")
                            b.waitg(sy, ("ptst", b.cnt["ptst"]))
                            wv1 = hdma(pr_r[:, :], part_dram.ap()[tok0:tok0 + CH, :])
                            wv2 = hdma(x0r[:, :], x0[3 + tok0:3 + tok0 + CH, :])
                            b.waitg(ve, wv1)
                            b.waitg(ve, wv2)
                            b.wait(ve, "t", tv)
                            V(ve, "v")
                            for q in range(4):
                                b.inc(ve.tensor_tensor(pr_r[:, q * 512:(q + 1) * 512],
                                                       pr_r[:, q * 512:(q + 1) * 512],
                                                       pk[q][:, 0:512], ALU.add), "v")
                            b.waitg(ve, r_hw)
                            b.inc(ve.scalar_tensor_tensor(x1[:, tc * D:(tc + 1) * D], pr_r[:, :],
                                                          rcols[:, tc:tc + 1], x0r[:, :],
                                                          ALU.mult, ALU.add), "v")
                            b.wait(te, "v")
            fence()
            v_x1 = b.val("v")
            if "x1t" in taps:
                b.wait(sy, "v", v_x1)
                tap_dma(taps["x1t"], x1[:, :])
            stop("F")
            yh_cm.__exit__(None, None, None)

            mark("G")
            with ExitStack() as sg:
                sbg = mk(sg)
                rowb = sbg("rowb", [1, D], F32)
                rowc = sbg("rowc", [1, D], F32)
                thr = sbg("thr", [1, D], F32)
                fmv = sbg("fmv", [128, 16], F32)
                fmbf = sbg("fmbf", [128, 16], BF16)
                gfm = sbg("gfm", [128, 8], BF16)
                grow = sbg("grow", [1, HS], BF16)
                bcbt = sbg("bcbt", [128, D], BF16, side="right")
                ogt = sbg("ogt", [128, D], BF16, side="right")
                obt = sbg("obt", [128, D], BF16, side="right")
                sgw = ExitStack()
                sbgw = mk(sgw)
                g_ing = sbgw("g_ing", [1, D], BF16)
                g_inb = sbgw("g_inb", [1, D], BF16)
                g_png = sbgw("g_png", [1, D], BF16)
                g_pnb = sbgw("g_pnb", [1, D], BF16)
                g_ab = sbgw("g_ab", [1, D], BF16)
                g_d2b = sbgw("g_d2b", [1, D], BF16)
                aggw_sb = sbgw("aggw_sb", [128, KC * AC_], BF16)
                d1w_sb = sbgw("d1w_sb", [128, KC * HS], BF16)
                d2w_sb = sbgw("d2w_sb", [128, 8 * D], BF16)
                d1b_bf = sbgw("d1b_bf", [1, HS], BF16)

                b.wait(gp, "v", v_x1)
                wv_lt = gdma(ltok_in.ap(), x1[127:128, (NCH - 1) * D:NCH * D])
                b.waitg(gp, wv_lt)
                b.inc(gp.collective_compute("AllGather", ALU.bypass,
                                            replica_groups=[list(range(NCORES))],
                                            ins=[ltok_in.ap().opt()], outs=[ltok_out.ap().opt()]),
                      "cc")
                b.wait(sy, "cc")
                wv_rows = hdma(rowa[0:1, :], ltok_out.ap()[NCORES - 1:NCORES, :])
                wv_rows_sw = _dma("sw", [(g_ing[0:1, :], in_gamma),
                                         (g_inb[0:1, :], in_beta),
                                         (g_png[0:1, :], pn_gamma),
                                         (g_pnb[0:1, :], pn_beta),
                                         (g_ab[0:1, :], agg_b),
                                         (g_d2b[0:1, :], d2_b)])

                def ln_row(src, dst, gamma=None, beta=None, eps=EPS_FFN, heron=True):
                    epsc = epsln_c if eps == EPS_LN else epsffn_c
                    ln_acc(src, dst, 1, D, epsc, lut=not heron, eps_f=eps)
                    b.wait(ve, "a")
                    b.wait(ve, "v")
                    V(ve, "v")
                    b.inc(ve.tensor_scalar(dst, src, nm[0:1, 0:1], rsc[0:1, 0:1],
                                           ALU.add, ALU.mult), "v")
                    if gamma is not None:
                        b.inc(ve.tensor_tensor(dst, dst, gamma, ALU.mult), "v")
                        b.inc(ve.tensor_tensor(dst, dst, beta, ALU.add), "v")

                def row_to_fm(src_row, dst_fm, n):
                    b.wait(sy, "v")
                    g1 = hdma(row_rr.ap()[0:1, 0:n * 128], src_row)
                    b.waitg(sy, g1)
                    return hdma(dst_fm,
                                row_rr.ap()[0:1, 0:n * 128].rearrange("o (k p) -> (o p) k", p=128),
                                slow=True)

                b.waitg(ve, wv_rows)
                b.waitg(ve, wv_rows_sw)
                b.waitg(ac, wv_rows)
                b.waitg(ac, wv_rows_sw)
                ln_row(rowa[0:1, :], rowb[0:1, :], eps=EPS_LN, heron=False)
                ln_row(rowb[0:1, :], rowc[0:1, :], gamma=g_ing[0:1, :], beta=g_inb[0:1, :],
                       eps=EPS_FFN, heron=False)
                wv = row_to_fm(rowc[0:1, :], fmv[:, 0:16], 16)
                b.waitg(ve, wv)
                b.inc(ve.tensor_copy(fmbf[:, 0:16], fmv[:, 0:16]), "v")
                wv = gdma(aggw_sb[:, :].rearrange("p (kc n) -> p kc n", kc=KC), agg_w.rearrange("(kc p) n -> p kc n", p=128))
                b.waitg(te, wv)
                b.wait(te, "v")
                mm = None
                for kc in range(KC):
                    mm = te.matmul(pk[0][0:1, 0:AC_], fmbf[:, kc:kc + 1],
                                   aggw_sb[:, kc * AC_:(kc + 1) * AC_],
                                   start=(kc == 0), stop=(kc == KC - 1))
                b.inc(mm, "t")
                b.wait(ve, "t")
                b.inc(ve.tensor_copy(rowb[0:1, 0:AC_], pk[0][0:1, 0:AC_]), "v")
                b.wait(sy, "v")
                wv_ag = hdma(agg_io.ap(), rowb[0:1, 0:AC_])
                b.waitg(gp, wv_ag)
                b.inc(gp.collective_compute("AllGather", ALU.bypass,
                                            replica_groups=[list(range(NCORES))],
                                            ins=[agg_io.ap().opt()], outs=[agg_go.ap().opt()]),
                      "cc")
                b.wait(sy, "cc")
                wv = hdma(thr[0:1, :], agg_go.ap().rearrange("r n -> (r n)").unsqueeze(0))
                b.waitg(ve, wv)
                b.inc(ve.tensor_tensor(thr[0:1, :], thr[0:1, :], g_ab[0:1, :], ALU.add), "v")

                wv_d1 = gdma(d1w_sb[:, :].rearrange("p (kc n) -> p kc n", kc=KC), d1_w.rearrange("(kc p) n -> p kc n", p=128))
                wv_d2 = gdma(d2w_sb[:, :].rearrange("p (k2 n) -> p k2 n", k2=8), d2_w.rearrange("(k2 p) n -> p k2 n", p=128))
                wv_d1b = gdma(d1b_bf[:, :], d1_b)
                for it in range(4):
                    ln_row(thr[0:1, :], rowb[0:1, :], gamma=g_png[0:1, :], beta=g_pnb[0:1, :],
                           eps=EPS_FFN, heron=True)
                    wv = row_to_fm(rowb[0:1, :], fmv[:, 0:16], 16)
                    b.waitg(ve, wv)
                    b.inc(ve.tensor_copy(fmbf[:, 0:16], fmv[:, 0:16]), "v")
                    b.wait(te, "v")
                    b.waitg(te, wv_d1)
                    b.waitg(te, wv_d2)
                    b.waitg(te, wv_d1b)
                    for half in range(2):
                        for kc in range(KC):
                            te.matmul(pk[half][0:1, 0:512], fmbf[:, kc:kc + 1],
                                      d1w_sb[:, kc * HS + half * 512:kc * HS + (half + 1) * 512],
                                      start=(kc == 0), stop=False)
                        mm = te.matmul(pk[half][0:1, 0:512], one11[:, :],
                                       d1b_bf[0:1, half * 512:(half + 1) * 512],
                                       start=False, stop=True)
                        b.inc(mm, "t")
                    b.wait(ac, "t")
                    V(ac, "a")
                    b.inc(ac.activation(grow[0:1, 0:512], pk[0][0:1, 0:512], AF.Gelu), "a")
                    b.inc(ac.activation(grow[0:1, 512:1024], pk[1][0:1, 0:512], AF.Gelu), "a")
                    b.wait(sy, "a")
                    g1 = hdma(row_rr_bf.ap()[0:1, 0:HS], grow[0:1, :])
                    b.waitg(sy, g1)
                    wv = hdma(gfm[:, 0:8],
                              row_rr_bf.ap()[0:1, 0:HS].rearrange("o (k p) -> (o p) k", p=128),
                              slow=True)
                    b.waitg(te, wv)
                    for q in range(4):
                        mm = None
                        for k2 in range(8):
                            mm = te.matmul(pk[q + 2][0:1, 0:512], gfm[:, k2:k2 + 1],
                                           d2w_sb[:, k2 * D + q * 512:k2 * D + (q + 1) * 512],
                                           start=(k2 == 0), stop=(k2 == 7))
                        b.inc(mm, "t")
                    b.wait(ve, "t")
                    V(ve, "v")
                    for q in range(4):
                        b.inc(ve.tensor_copy(rowb[0:1, q * 512:(q + 1) * 512],
                                             pk[q + 2][0:1, 0:512]), "v")
                    b.wait(sy, "v")
                    wv_d2p = hdma(d2p_io.ap(), rowb[0:1, :])
                    b.waitg(gp, wv_d2p)
                    b.inc(gp.collective_compute("AllReduce", ALU.add,
                                                replica_groups=[list(range(NCORES))],
                                                ins=[d2p_io.ap().opt()], outs=[d2p_go.ap().opt()]),
                          "cc")
                    b.wait(sy, "cc")
                    wv = hdma(rowb[0:1, :], d2p_go.ap())
                    b.waitg(ve, wv)
                    b.inc(ve.tensor_tensor(rowb[0:1, :], rowb[0:1, :], g_d2b[0:1, :], ALU.add), "v")
                    b.inc(ve.tensor_tensor(thr[0:1, :], thr[0:1, :], rowb[0:1, :], ALU.add), "v")
                if "thought" in taps:
                    b.wait(sy, "v")
                    tap_dma(taps["thought"], thr[0:1, :])
                b.wait(gp, "t")
                wv = gdma(aggw_sb[:, :].rearrange("p (kc n) -> p kc n", kc=KC), bc_w.rearrange("(kc p) n -> p kc n", p=128))
                wv2 = row_to_fm(thr[0:1, :], fmv[:, 0:16], 16)
                b.waitg(ve, wv2)
                b.inc(ve.tensor_copy(fmbf[:, 0:16], fmv[:, 0:16]), "v")
                b.waitg(te, wv)
                b.wait(te, "v")
                mm = None
                for kc in range(KC):
                    mm = te.matmul(pk[0][0:1, 0:AC_], fmbf[:, kc:kc + 1],
                                   aggw_sb[:, kc * AC_:(kc + 1) * AC_],
                                   start=(kc == 0), stop=(kc == KC - 1))
                b.inc(mm, "t")
                b.wait(ve, "t")
                b.inc(ve.tensor_copy(rowb[0:1, 0:AC_], pk[0][0:1, 0:AC_]), "v")
                fence()
                sgw.close()
                b.wait(sy, "v")
                wv_bc = hdma(bc_io.ap(), rowb[0:1, 0:AC_])
                b.waitg(gp, wv_bc)
                b.inc(gp.collective_compute("AllGather", ALU.bypass,
                                            replica_groups=[list(range(NCORES))],
                                            ins=[bc_io.ap().opt()], outs=[bc_go.ap().opt()]),
                      "cc")
                b.wait(sy, "cc")
                wv = hdma(rowb[0:1, :], bc_go.ap().rearrange("r n -> (r n)").unsqueeze(0))
                wv2 = hdma(rowc[0:1, :], bc_b)
                b.waitg(ve, wv)
                b.waitg(ve, wv2)
                b.inc(ve.tensor_tensor(rowb[0:1, :], rowb[0:1, :], rowc[0:1, :], ALU.add), "v")
                b.inc(ve.tensor_copy(rowbf[0:1, :], rowb[0:1, :]), "v")
                for ridx, dst in [(0, bcbt), (1, ogt), (2, obt)]:
                    if ridx == 1:
                        wv3 = hdma(rowa[0:1, :], out_gamma)
                        b.waitg(ve, wv3)
                        b.inc(ve.tensor_copy(rowbf[0:1, :], rowa[0:1, :]), "v")
                    if ridx == 2:
                        wv3 = hdma(rowa[0:1, :], out_beta)
                        b.waitg(ve, wv3)
                        b.inc(ve.tensor_copy(rowbf[0:1, :], rowa[0:1, :]), "v")
                    b.wait(te, "v")
                    for q in range(4):
                        b.inc(te.matmul(pk[q][:, 0:512], ones_sb[0:1, :],
                                        rowbf[0:1, q * 512:(q + 1) * 512],
                                        start=True, stop=True), "t")
                    b.wait(ve, "t")
                    for q in range(4):
                        b.inc(ve.tensor_copy(dst[:, q * 512:(q + 1) * 512], pk[q][:, 0:512]), "v")
                    b.wait(te, "v")
                    if ridx == 1:
                        b.wait(sy, "v")

                mark("H")
                with ExitStack() as sh:
                    sbh = mk(sh)
                    gw_sb = sbh("gw_sb", [128, KC * D], BF16)
                    gsb = sbh("gsb", [128, NCH * D], BF16)
                    xftm = sbh("xftm", [128, D], BF16)
                    xffm = sbh("xffm", [128, KC * 128], BF16)
                    prh = sbh("prh", [128, D], F32)
                    outh = sbh("outh", [128, D], F32)
                    b.wait(gp, "t")
                    w_gws = _dma("sw", [(gw_sb[:, kb * D:(kb + 1) * D],
                                         gate_w[kb * 128:(kb + 1) * 128, :])
                                        for kb in range(KC)])
                    b.wait(sy, "v")
                    w_gsw = gdma(rowbf[0:1, :], gate_b)
                    for tc in range(NCH):
                        if "H" in FENCE_EVERY:
                            fence()
                        xc = x1[:, tc * D:(tc + 1) * D]
                        ln_acc(xc, xftm[:, :])
                        b.wait(ve, "a")
                        V(ve, "v")
                        b.inc(ve.tensor_scalar(xftm[:, :], xc, nm[:, 0:1], rsc[:, 0:1],
                                               ALU.add, ALU.mult), "v")
                        for half in range(2):
                            b.wait(te, "v")
                            for kk in range(8):
                                kc = half * 8 + kk
                                b.inc(te.transpose(pbf[:, kk * 128:(kk + 1) * 128],
                                                   xftm[:, kc * 128:(kc + 1) * 128],
                                                   ident_sb[:, :]), "t")
                            b.wait(ve, "t")
                            V(ve, "v")
                            b.inc(ve.tensor_copy(xffm[:, half * 1024:(half + 1) * 1024],
                                                 pbf[:, :]), "v")
                            b.wait(te, "v")
                        b.waitg(te, w_gws)
                        b.waitg(te, w_gsw)
                        b.wait(te, "v")
                        for kc in range(KC):
                            for q in range(4):
                                te.matmul(pk[q][:, 0:512], xffm[:, kc * 128:(kc + 1) * 128],
                                          gw_sb[:, kc * D + q * 512:kc * D + (q + 1) * 512],
                                          start=(kc == 0), stop=False)
                        mm = None
                        for q in range(4):
                            mm = te.matmul(pk[q][:, 0:512], ones_sb[0:1, :],
                                           rowbf[0:1, q * 512:(q + 1) * 512],
                                           start=False, stop=True)
                        b.inc(mm, "t")
                        b.wait(ac, "t")
                        V(ac, "a")
                        for q in range(4):
                            b.inc(ac.activation(gsb[:, tc * D + q * 512:tc * D + (q + 1) * 512],
                                                pk[q][:, 0:512], AF.Sigmoid), "a")
                        b.wait(te, "a")
                    if "gate" in taps:
                        b.wait(sy, "a")
                        tap_dma(taps["gate"], gsb[:, :])
                    for tc in range(NCH):
                        xc = x1[:, tc * D:(tc + 1) * D]
                        ln_acc(xc, xftm[:, :])
                        b.wait(ve, "a")
                        V(ve, "v")
                        b.inc(ve.tensor_scalar(xftm[:, :], xc, nm[:, 0:1], rsc[:, 0:1],
                                               ALU.add, ALU.mult), "v")
                        b.inc(ve.tensor_tensor(prh[:, :], gsb[:, tc * D:(tc + 1) * D],
                                               bcbt[:, :], ALU.mult), "v")
                        b.inc(ve.tensor_tensor(prh[:, :], prh[:, :], xftm[:, :], ALU.add), "v")
                        ln_acc(prh[:, :], xftm[:, :], epsc=epsffn_c)
                        b.wait(ve, "a")
                        V(ve, "v")
                        b.inc(ve.tensor_scalar(prh[:, :], prh[:, :], nm[:, 0:1], rsc[:, 0:1],
                                               ALU.add, ALU.mult), "v")
                        b.inc(ve.tensor_tensor(prh[:, :], prh[:, :], ogt[:, :], ALU.mult), "v")
                        b.inc(ve.tensor_tensor(prh[:, :], prh[:, :], obt[:, :], ALU.add), "v")
                        sto = outh[:, 0:D]
                        b.waitg(ve, ("outst", b.cnt["outst"]))
                        b.inc(ve.tensor_tensor(sto, prh[:, :], xc, ALU.add), "v")
                        b.wait(sy, "v")
                        hdma(out[tc * CH:(tc + 1) * CH, :], sto, sem="outst")
            mark(None)
            b.wait(sy, "outst", b.cnt["outst"])

    return nc


def _stage_inputs(I, T=1024):
    import ml_dtypes
    f = np.float32
    bf = ml_dtypes.bfloat16
    x = np.asarray(I["L0_attn_hidden_states"], f).reshape(-1, D)
    ipw = np.asarray(I["L0_attn_in_proj_w"], f)
    w_hbcdt = np.ascontiguousarray(ipw[:, DI:DI + HBCW])
    w_gate = np.ascontiguousarray(ipw[:, 0:DI])
    w_out = np.ascontiguousarray(np.asarray(I["L0_attn_out_proj_w"], f))
    cw = np.asarray(I["L0_attn_conv_w"], f)
    conv_w = np.ascontiguousarray(cw.reshape(34, 128, 4).transpose(1, 0, 2).reshape(128, 136))
    conv_b = np.ascontiguousarray(np.asarray(I["L0_attn_conv_b"], f).reshape(34, 128).T)
    dt_bias = np.asarray(I["L0_attn_dt_bias"], f).reshape(64, 1)
    a_log = np.asarray(I["L0_attn_A_log"], f).reshape(64, 1)
    d_row = np.asarray(I["L0_attn_D_param"], f).reshape(1, H)
    norm_w = np.ascontiguousarray(np.asarray(I["L0_attn_norm_w"], f).reshape(32, 128).T)
    row = lambda k: np.ascontiguousarray(np.asarray(I[k], f).reshape(1, -1))
    n1 = np.arange(128)
    consts = {
        "ones_r": np.ones((1, 128), bf),
        "ones2_r": np.ones((2, 128), bf),
        "nones2_r": -np.ones((2, 128), bf),
        "tri_r": (n1[:, None] < n1[None, :]).astype(bf),
        "negI_r": (-1e30 * np.eye(128)).astype(bf),
        "ident_r": np.eye(128, dtype=bf),
        "identf_r": np.eye(128, dtype=f),
        "onescol_r": np.ones((128, 1), bf),
    }
    HSn = HFF // NCORES
    ACn = D // NCORES
    in_maps = []
    for k in range(NCORES):
        t0 = k * T
        xs = np.zeros((T + 3, D), f)
        lo = max(0, t0 - 3)
        xs[3 - (t0 - lo):3] = x[lo:t0]
        xs[3:] = x[t0:t0 + T]
        j = np.arange(NCORES - 1)
        m = {
            "x0": xs, "w_hbcdt": w_hbcdt, "w_gate": w_gate, "w_out": w_out,
            "conv_w": conv_w, "conv_b": conv_b, "dt_bias": dt_bias,
            "a_log": a_log, "d_row": d_row, "norm_w": norm_w,
            "in_gamma": row("L0_ffn_in_gamma"), "in_beta": row("L0_ffn_in_beta"),
            "agg_w": np.ascontiguousarray(np.asarray(I["L0_ffn_agg_w"], f)[:, k * ACn:(k + 1) * ACn]),
            "agg_b": row("L0_ffn_agg_b"),
            "d1_w": np.ascontiguousarray(np.asarray(I["L0_ffn_d1_w"], f)[:, k * HSn:(k + 1) * HSn]),
            "d1_b": np.ascontiguousarray(row("L0_ffn_d1_b")[:, k * HSn:(k + 1) * HSn]),
            "d2_w": np.ascontiguousarray(np.asarray(I["L0_ffn_d2_w"], f)[k * HSn:(k + 1) * HSn, :]),
            "d2_b": row("L0_ffn_d2_b"),
            "pn_gamma": row("L0_ffn_pn_gamma"), "pn_beta": row("L0_ffn_pn_beta"),
            "bc_w": np.ascontiguousarray(np.asarray(I["L0_ffn_bc_w"], f)[:, k * ACn:(k + 1) * ACn]),
            "bc_b": row("L0_ffn_bc_b"),
            "gate_w": np.ascontiguousarray(np.asarray(I["L0_ffn_gate_w"], f)),
            "gate_b": row("L0_ffn_gate_b"),
            "out_gamma": row("L0_ffn_out_gamma"), "out_beta": row("L0_ffn_out_beta"),
            "sel2_r": np.ascontiguousarray(np.broadcast_to(
                np.concatenate([(j < k).astype(f), (j <= k - 1).astype(f)]),
                (64, 2 * (NCORES - 1)))),
        }
        m.update(consts)
        in_maps.append(m)
    return in_maps


def kernel(**inputs):
    T = 1024
    in_maps = _stage_inputs(inputs, T)
    nc = build_program(T)
    res = run_bass_kernel_spmd(nc, in_maps, core_ids=list(range(NCORES)))
    outp = np.concatenate([np.asarray(r["out"], np.float32) for r in res.results], axis=0)
    return outp.reshape(1, NCORES * T, D)


# version 08 (final)
# speedup vs baseline: 1.0057x; 1.0057x over previous
import numpy as np
from contextlib import ExitStack

import concourse.bass as bass
import concourse.mybir as mybir
from concourse.bass_utils import run_bass_kernel_spmd

F32 = mybir.dt.float32
BF16 = mybir.dt.bfloat16
AF = mybir.ActivationFunctionType
ALU = mybir.AluOpType

NCORES = 8
D = 2048
DI = 4096
NST = 128
H = 64
HBCW = 4416
KC = D // 128
CH = 128
HFF = 8192
EPS_LN = 1e-5
EPS_FFN = 1e-6

SAFE_EDGES = False
FENCE_EVERY = set()
DEBUG_TAPS = set()
STOP_AFTER = None


class _StopBuild(Exception):
    pass


class KB:
    def __init__(self, nc):
        self.nc = nc
        self.cnt = {}
        self.sems = {}
        self.last = {}

    def sem(self, st, name):
        self.sems[name] = st.enter_context(self.nc.semaphore("s_" + name))
        self.cnt[name] = 0
        self.last[name] = {}

    def inc(self, inst, sem, amt=1):
        inst.then_inc(self.sems[sem], amt)
        self.cnt[sem] += amt
        return self.cnt[sem]

    def val(self, sem):
        return self.cnt[sem]

    def wait(self, eng, sem, v=None):
        if v is None:
            v = self.cnt[sem]
        if v <= 0:
            return
        key = str(getattr(eng, "engine", id(eng)))
        if self.last[sem].get(key, 0) >= v:
            return
        eng.wait_ge(self.sems[sem], v)
        self.last[sem][key] = v

    def waitg(self, eng, grp):
        if grp is None:
            return
        self.wait(eng, grp[0], grp[1])


def build_program(T=1024):
    NCH = T // CH
    S = T + 3
    PS = min(512, T // 2)
    NPS = T // PS
    HS = HFF // NCORES
    AC_ = D // NCORES

    nc = bass.Bass("TRN2", target_bir_lowering=False, debug=False)

    def din(name, shape, dt=F32):
        return nc.dram_tensor(name, shape, dt, kind="ExternalInput").ap()

    x0 = din("x0", [S, D])
    w_hbcdt = din("w_hbcdt", [D, HBCW])
    w_gate = din("w_gate", [D, DI])
    w_out = din("w_out", [DI, D])
    conv_w = din("conv_w", [128, 34 * 4])
    conv_b = din("conv_b", [128, 34])
    dt_bias = din("dt_bias", [64, 1])
    a_log = din("a_log", [64, 1])
    d_row = din("d_row", [1, H])
    norm_w = din("norm_w", [128, 32])
    in_gamma = din("in_gamma", [1, D])
    in_beta = din("in_beta", [1, D])
    agg_w = din("agg_w", [D, AC_])
    agg_b = din("agg_b", [1, D])
    d1_w = din("d1_w", [D, HS])
    d1_b = din("d1_b", [1, HS])
    d2_w = din("d2_w", [HS, D])
    d2_b = din("d2_b", [1, D])
    pn_gamma = din("pn_gamma", [1, D])
    pn_beta = din("pn_beta", [1, D])
    bc_w = din("bc_w", [D, AC_])
    bc_b = din("bc_b", [1, D])
    gate_w = din("gate_w", [D, D])
    gate_b = din("gate_b", [1, D])
    out_gamma = din("out_gamma", [1, D])
    out_beta = din("out_beta", [1, D])
    ones_r = din("ones_r", [1, 128], BF16)
    ones2_r = din("ones2_r", [2, 128], BF16)
    nones2_r = din("nones2_r", [2, 128], BF16)
    tri_r = din("tri_r", [128, 128], BF16)
    negI_r = din("negI_r", [128, 128], BF16)
    ident_r = din("ident_r", [128, 128], BF16)
    identf_r = din("identf_r", [128, 128], F32)
    onescol_r = din("onescol_r", [128, 1], BF16)
    sel2_r = din("sel2_r", [64, 2 * (NCORES - 1)])

    out = nc.dram_tensor("out", [T, D], F32, kind="ExternalOutput").ap()

    def scratch(name, shape, dt):
        kind = "ExternalOutput" if name in DEBUG_TAPS else "Internal"
        return nc.dram_tensor(name, shape, dt, kind=kind)

    hbc_dram = scratch("hbc_dram", [34, 128, S], BF16)
    prevL_dram = scratch("prevL_dram", [NCH * NST, DI], BF16)
    part_dram = scratch("part_dram", [T, D], F32)
    st_in = nc.dram_tensor("st_in", [NST, DI + 64], BF16)
    st_out = scratch("st_out", [NCORES * NST, DI + 64], BF16)
    ltok_in = nc.dram_tensor("ltok_in", [1, D], F32)
    ltok_out = nc.dram_tensor("ltok_out", [NCORES, D], F32)
    agg_io = nc.dram_tensor("agg_io", [1, AC_], F32)
    agg_go = nc.dram_tensor("agg_go", [NCORES, AC_], F32)
    d2p_io = nc.dram_tensor("d2p_io", [1, D], F32)
    d2p_go = nc.dram_tensor("d2p_go", [1, D], F32)
    bc_io = nc.dram_tensor("bc_io", [1, AC_], F32)
    bc_go = nc.dram_tensor("bc_go", [NCORES, AC_], F32)
    row_rr = nc.dram_tensor("row_rr", [1, HFF], F32)
    row_rr_bf = nc.dram_tensor("row_rr_bf", [1, HFF], BF16)
    taps = {}
    for t_ in DEBUG_TAPS:
        if t_ == "xn1":
            taps["xn1"] = nc.dram_tensor("tap_xn1", [128, KC * S], BF16, kind="ExternalOutput").ap()
        if t_ == "cum":
            taps["cum"] = nc.dram_tensor("tap_cum", [64, NCH * CH], F32, kind="ExternalOutput").ap()
        if t_ == "yh0":
            taps["yh0"] = nc.dram_tensor("tap_yh0", [128, 32 * T], BF16, kind="ExternalOutput").ap()
        if t_ == "yh":
            taps["yh"] = nc.dram_tensor("tap_yh", [128, 32 * T], BF16, kind="ExternalOutput").ap()
        if t_ == "hidden":
            taps["hidden"] = nc.dram_tensor("tap_hidden", [128, 32 * T], BF16, kind="ExternalOutput").ap()
        if t_ == "x1t":
            taps["x1t"] = nc.dram_tensor("tap_x1t", [128, NCH * D], BF16, kind="ExternalOutput").ap()
        if t_ == "gate":
            taps["gate"] = nc.dram_tensor("tap_gate", [128, NCH * D], BF16, kind="ExternalOutput").ap()
        if t_ == "thought":
            taps["thought"] = nc.dram_tensor("tap_thought", [1, D], F32, kind="ExternalOutput").ap()
        if t_ == "dtf":
            taps["dtf"] = nc.dram_tensor("tap_dtf", [64, S], F32, kind="ExternalOutput").ap()
        if t_ == "whm":
            taps["whm"] = nc.dram_tensor("tap_whm", [64, 4 * (NCORES - 1)], F32, kind="ExternalOutput").ap()
        if t_ == "selv":
            taps["selv"] = nc.dram_tensor("tap_selv", [64, 2 * (NCORES - 1)], F32, kind="ExternalOutput").ap()
        if t_ == "rows7":
            taps["rows7"] = nc.dram_tensor("tap_rows7", [NCORES - 1, 64], F32, kind="ExternalOutput").ap()
        if t_ == "wrep":
            taps["wrep"] = nc.dram_tensor("tap_wrep", [128, (NCORES - 1) * H], F32, kind="ExternalOutput").ap()
        if t_ == "incs":
            taps["incs"] = nc.dram_tensor("tap_incs", [128, DI], BF16, kind="ExternalOutput").ap()

    b = KB(nc)

    _WRAP_OPS = {"tensor_copy", "tensor_tensor", "tensor_scalar", "tensor_add",
                 "tensor_mul", "tensor_sub", "tensor_max", "tensor_relu",
                 "scalar_tensor_tensor", "tensor_tensor_scan", "tensor_reduce",
                 "reduce_sum", "reduce_max", "bn_stats", "bn_aggr", "memset",
                 "reciprocal", "activation", "copy", "add", "mul",
                 "tensor_scalar_add", "tensor_scalar_mul", "tensor_single_scalar"}

    class SafeEng:

        def __init__(self, eng, sem):
            self._eng = eng
            self._sem = sem
            self.engine = getattr(eng, "engine", None)

        def __getattr__(self, name):
            f = getattr(self._eng, name)
            if name in _WRAP_OPS:
                def g(*a, **k):
                    small = False
                    if a and name != "scalar_tensor_tensor":
                        fs = getattr(a[0], "free_size", None)
                        if callable(fs):
                            fs = fs()
                        small = fs is not None and fs < 128
                    if SAFE_EDGES or small:
                        b.wait(self._eng, self._sem)
                    return f(*a, **k)
                return g
            return f

    sy, gp, te = nc.sync, nc.gpsimd, nc.tensor
    ve = SafeEng(nc.vector, "v")
    ac = SafeEng(nc.scalar, "a")

    def V(eng, sem):
        if SAFE_EDGES:
            b.wait(eng, sem)

    with ExitStack() as g0:
        def mk(stk):
            def sb(name, shape, dt, side=None):
                return stk.enter_context(nc.sbuf_tensor(name, shape, dt, side=side))
            return sb

        sb0 = mk(g0)
        ones_sb = sb0("ones_sb", [1, 128], BF16)
        ident_sb = sb0("ident_sb", [128, 128], BF16)
        onescol_sb = sb0("onescol_sb", [128, 1], BF16)
        nwc = sb0("nwc", [128, 32], F32)
        one11 = sb0("one11", [1, 1], BF16)
        rowa = sb0("rowa", [1, D], F32)
        rowbf = sb0("rowbf", [1, D], BF16)
        snm = sb0("snm", [128, 1], F32)
        nm = sb0("nm", [128, 1], F32)
        sq2 = sb0("sq2", [128, 1], F32)
        rsc = sb0("rsc", [128, 1], F32)
        hr_v = sb0("hr_v", [128, 1], F32)
        hr_y = sb0("hr_y", [128, 1], F32)
        hr_t = sb0("hr_t", [128, 1], F32)
        hr_v2 = sb0("hr_v2", [128, 1], F32)
        s11a = sb0("s11a", [1, 1], F32)
        s11b = sb0("s11b", [1, 1], F32)
        s11c = sb0("s11c", [1, 1], F32)
        epsln_c = sb0("epsln_c", [128, 1], F32)
        epsffn_c = sb0("epsffn_c", [128, 1], F32)
        onef_c = sb0("onef_c", [128, 1], F32)
        rcols = sb0("rcols", [128, NCH], F32)
        pk = [g0.enter_context(nc.psum_tensor(f"pk{i}", [128, 512], F32)) for i in range(7)]
        pbf = g0.enter_context(nc.psum_tensor("pbf", [128, 1024], BF16))
        NPOOL = 6
        for name in (["cc", "t", "v", "a", "stg0", "stg1", "stgt", "plst", "ptst", "outst"]
                     + [f"hw{i}" for i in range(NPOOL)]
                     + [f"sw{i}" for i in range(NPOOL)]):
            b.sem(g0, name)
        rr = {"hw": 0, "sw": 0}

        def _dma(queue, pairs, sem=None, slow=False):
            eng = sy if queue == "hw" else gp
            if sem is None:
                sem = f"{queue}{rr[queue] % NPOOL}"
                rr[queue] += 1
                if b.cnt[sem] > 0:
                    b.wait(eng, sem, b.cnt[sem])
            for dst, src_ in pairs:
                if slow:
                    inst = eng.dma_start(out=dst, in_=src_, allow_slow_non_contiguous=True)
                else:
                    inst = eng.dma_start(out=dst, in_=src_)
                b.inc(inst, sem, 16)
            return (sem, b.cnt[sem])

        def fence():
            sems = ["t", "v", "a", "stg0", "stg1", "stgt", "plst", "ptst", "outst"] + \
                   [f"hw{i}" for i in range(NPOOL)] + [f"sw{i}" for i in range(NPOOL)]
            for eng in (sy, gp, ve, ac, te):
                for s_ in sems:
                    b.wait(eng, s_, b.cnt[s_])

        _scope = [None, None]

        def mark(name):
            if _scope[0] is not None:
                nc.leave_named_scope(_scope[0], _scope[1], False)
                _scope[0] = None
            if name:
                sid, _ = nc.enter_named_scope(name, False)
                _scope[0], _scope[1] = name, sid

        def stop(tag):
            if STOP_AFTER == tag:
                fence()
                for eng in (sy, gp, nc.vector, nc.scalar, te):
                    b.wait(eng, "cc", b.cnt["cc"])
                raise _StopBuild(nc)

        def hdma(dst, src, sem=None, slow=False):
            return _dma("hw", [(dst, src)], sem=sem, slow=slow)

        def tap_dma(dst, src):
            grp = _dma("hw", [(dst, src)])
            for e_ in (nc.vector, nc.scalar, nc.tensor, gp):
                b.waitg(e_, grp)

        def gdma(dst, src, sem=None):
            return _dma("sw", [(dst, src)], sem=sem)

        def ln_acc(src_ap, sink_ap, rows=128, width=D, epsc=None, lut=True,
                   eps_f=EPS_LN):
            if epsc is None:
                epsc = epsln_c
            b.wait(ac, "v")
            V(ac, "a")
            b.inc(ac.activation(sink_ap, src_ap, AF.Copy, accum_out=snm[0:rows, :]), "a")
            b.wait(ve, "a")
            V(ve, "v")
            b.inc(ve.tensor_scalar(nm[0:rows, :], snm[0:rows, :], -1.0 / width,
                                   None, ALU.mult), "v")
            b.wait(ac, "v")
            b.inc(ac.activation(sink_ap, src_ap, AF.Square, bias=nm[0:rows, 0:1],
                                accum_out=sq2[0:rows, :]), "a")
            if lut:
                b.inc(ac.activation(rsc[0:rows, :], sq2[0:rows, :], AF.Ln,
                                    scale=1.0 / width, bias=epsc[0:rows, 0:1]), "a")
                b.inc(ac.activation(rsc[0:rows, :], rsc[0:rows, :], AF.Exp, scale=-0.5), "a")
            else:
                b.wait(ve, "a")
                V(ve, "v")
                b.inc(ve.tensor_scalar(hr_v[0:rows, :], sq2[0:rows, :], 1.0 / width,
                                       eps_f, ALU.mult, ALU.add), "v")
                b.inc(ve.reciprocal(hr_y[0:rows, :], hr_v[0:rows, :]), "v")
                b.inc(ve.tensor_copy(hr_t[0:rows, :], hr_y[0:rows, :]), "v")
                for _ in range(10):
                    b.inc(ve.reciprocal(hr_v2[0:rows, :], hr_t[0:rows, :]), "v")
                    b.inc(ve.tensor_tensor(hr_v2[0:rows, :], hr_y[0:rows, :],
                                           hr_v2[0:rows, :], ALU.mult), "v")
                    b.inc(ve.tensor_tensor(hr_t[0:rows, :], hr_t[0:rows, :],
                                           hr_v2[0:rows, :], ALU.add), "v")
                    b.inc(ve.tensor_scalar(hr_t[0:rows, :], hr_t[0:rows, :], 0.5,
                                           None, ALU.mult), "v")
                b.inc(ve.tensor_copy(rsc[0:rows, :], hr_t[0:rows, :]), "v")

        yh_cm = nc.sbuf_tensor("yh", [128, 32 * T], BF16, side="right")
        yh = yh_cm.__enter__()

        with ExitStack() as s1:
            sb1 = mk(s1)
            ones2_sb = sb1("ones2_sb", [2, 128], BF16)
            nones2_sb = sb1("nones2_sb", [2, 128], BF16)
            tri_sb = sb1("tri_sb", [128, 128], BF16)
            negI_sb = sb1("negI_sb", [128, 128], BF16)
            identf_sb = sb1("identf_sb", [128, 128], F32)
            convw_sb = sb1("convw_sb", [128, 34 * 4], F32)
            convb_sb = sb1("convb_sb", [128, 34], F32)
            dtb_sb = sb1("dtb_sb", [64, 1], F32)
            acol = sb1("acol", [64, 1], F32)
            drep = sb1("drep", [128, H], F32)
            selv = sb1("selv", [64, 2 * (NCORES - 1)], F32)
            ones64 = sb1("ones64", [64, CH], F32)
            zcol = sb1("zcol", [64, 1], F32)
            x_n1 = sb1("x_n1", [128, KC * S], BF16)
            c_fm = sb1("c_fm", [128, T], BF16)
            cumS = sb1("cumS", [64, NCH * CH], F32)
            dprep = sb1("dprep", [128, NCH * H], F32)
            dt_fm = sb1("dt_fm", [64, S], F32)
            spl = sb1("spl", [64, S], F32)

            wv_const = _dma("hw", [(ones_sb[:, :], ones_r), (ones2_sb[:, :], ones2_r),
                             (nones2_sb[:, :], nones2_r), (tri_sb[:, :], tri_r),
                             (negI_sb[:, :], negI_r), (ident_sb[:, :], ident_r),
                             (identf_sb[:, :], identf_r), (onescol_sb[:, :], onescol_r),
                             (convw_sb[:, :], conv_w), (convb_sb[:, :], conv_b),
                             (dtb_sb[:, :], dt_bias), (acol[:, :], a_log),
                             (nwc[:, :], norm_w),
                             (selv[:, :], sel2_r)])
            wv_drow = gdma(rowbf[0:1, 0:H], d_row)
            b.waitg(ac, wv_const)
            b.inc(ac.activation(acol[:, :], acol[:, :], AF.Exp), "a")
            b.wait(ve, "a")
            V(ve, "v")
            b.inc(ve.tensor_scalar(acol[:, :], acol[:, :], -1.0, None, ALU.mult), "v")
            b.inc(ve.memset(ones64[:, :], 1.0), "v")
            b.inc(ve.memset(epsln_c[:, :], EPS_LN), "v")
            b.inc(ve.memset(epsffn_c[:, :], EPS_FFN), "v")
            b.inc(ve.memset(onef_c[:, :], 1.0), "v")
            b.inc(ve.memset(zcol[:, :], 0.0), "v")
            b.inc(ve.memset(one11[:, :], 1.0), "v")
            b.waitg(te, wv_drow)
            b.waitg(te, wv_const)
            b.inc(te.matmul(pk[0][:, 0:H], ones_sb[0:1, :], rowbf[0:1, 0:H],
                            start=True, stop=True), "t")
            b.wait(ve, "t")
            b.inc(ve.tensor_copy(drep[:, :], pk[0][:, 0:H]), "v")
            v_pre = b.val("v")

            mark("A")
            with ExitStack() as sa:
                sba = mk(sa)
                x0r = sba("x0r", [128, 2 * D], F32)
                xnt = sba("xnt", [128, D], BF16)
                NSC = (S + 127) // 128
                for sc in range(NSC):
                    rows = min(128, S - sc * 128)
                    xr = x0r[0:rows, (sc % 2) * D:(sc % 2) * D + D]
                    if sc >= 2:
                        b.wait(sy, "v")
                    wv = hdma(xr, x0[sc * 128:sc * 128 + rows, :])
                    b.waitg(ac, wv)
                    b.waitg(ve, wv)
                    ln_acc(xr, xnt[0:rows, :], rows)
                    b.wait(ve, "a")
                    V(ve, "v")
                    b.inc(ve.tensor_scalar(xnt[0:rows, :], xr, nm[0:rows, 0:1], rsc[0:rows, 0:1],
                                           ALU.add, ALU.mult), "v")
                    if "A" in FENCE_EVERY:
                        fence()
                    for half in range(2):
                        b.wait(te, "v")
                        for kk in range(8):
                            kc = half * 8 + kk
                            b.inc(te.transpose(pbf[:, kk * 128:kk * 128 + rows],
                                               xnt[0:rows, kc * 128:(kc + 1) * 128],
                                               ident_sb[0:rows, 0:rows]), "t")
                        b.wait(ve, "t")
                        b.inc(ve.tensor_copy(
                            x_n1[:, :].rearrange("p (kc s) -> p kc s", kc=KC)
                                [:, half * 8:(half + 1) * 8, sc * 128:sc * 128 + rows],
                            pbf[:, :].rearrange("p (kk s) -> p kk s", kk=8)[:, :, 0:rows]), "v")
                        b.wait(te, "v")
            fence()
            v_xn1 = b.val("v")
            if "xn1" in taps:
                b.wait(sy, "v", v_xn1)
                tap_dma(taps["xn1"], x_n1[:, :])
            stop("A")

            mark("B")
            with ExitStack() as sB:
                sbB = mk(sB)
                wring = sbB("wring", [128, 3 * 768], BF16)
                evst = sbB("evst", [128, 2 * PS + 4], BF16)
                tailacc = sbB("tailacc", [128, 35 * 3], F32)
                blocks = [(cb, 128) for cb in range(34)] + [(34, 64)]
                groups = [blocks[i:i + 6] for i in range(0, 35, 6)]
                ptail = pk[6]
                t_ring = {}
                stg_grp = {}
                b.inc(ve.memset(tailacc[:, :], 0.0), "v")
                for pi in range(NPS):
                    s0_, s1_ = pi * PS, (pi + 1) * PS
                    for gi, grp in enumerate(groups):
                        c0 = grp[0][0] * 128
                        cw = sum(w for _, w in grp)
                        b.wait(te, "v")
                        b.wait(te, "a")
                        if "B" in FENCE_EVERY:
                            fence()
                        for kc in range(KC):
                            ring = (gi * KC + kc) % 3
                            wt = wring[:, ring * 768:ring * 768 + cw]
                            if t_ring.get(ring) is not None:
                                b.wait(gp, "t", t_ring[ring])
                            wv = gdma(wt, w_hbcdt[kc * 128:(kc + 1) * 128, c0:c0 + cw])
                            b.waitg(te, wv)
                            b.wait(te, "v", v_xn1)
                            off = 0
                            mm = None
                            if pi == 0:
                                b.wait(te, "v")
                            for bi, (cb, wdt) in enumerate(grp):
                                mm = te.matmul(pk[bi][0:wdt, 0:PS], wt[:, off:off + wdt],
                                               x_n1[:, kc * S + s0_:kc * S + s1_],
                                               start=(kc == 0), stop=(kc == KC - 1))
                                if pi == 0:
                                    mm = te.matmul(
                                        ptail[0:wdt, bi * 3:bi * 3 + 3],
                                        wt[:, off:off + wdt],
                                        x_n1[:, kc * S + T:kc * S + T + 3],
                                        start=True, stop=True)
                                off += wdt
                            b.inc(mm, "t")
                            t_ring[ring] = b.val("t")
                            if pi == 0:
                                ncols = 3 * len(grp)
                                b.wait(ve, "t")
                                b.inc(ve.tensor_tensor(
                                    tailacc[:, grp[0][0] * 3:grp[0][0] * 3 + ncols],
                                    tailacc[:, grp[0][0] * 3:grp[0][0] * 3 + ncols],
                                    ptail[:, 0:ncols], ALU.add), "v")
                        tv = b.val("t")
                        for bi, (cb, wdt) in enumerate(grp):
                            if cb == 34:
                                b.wait(ve, "t", tv)
                                V(ve, "v")
                                b.inc(ve.tensor_copy(dt_fm[0:64, s0_:s1_], pk[bi][0:64, 0:PS]), "v")
                                if pi == NPS - 1:
                                    b.inc(ve.tensor_copy(
                                        dt_fm[0:64, T:T + 3],
                                        tailacc[0:64, 34 * 3:34 * 3 + 3]), "v")
                                continue
                            stg = evst[0:wdt, (bi % 2) * PS:(bi % 2) * PS + PS]
                            if bi % 2 == 0:
                                b.wait(ac, "t", tv)
                                b.waitg(ac, stg_grp.get(bi % 2))
                                V(ac, "a")
                                b.inc(ac.copy(stg, pk[bi][0:wdt, 0:PS]), "a")
                                b.wait(sy, "a")
                            else:
                                b.wait(ve, "t", tv)
                                b.waitg(ve, stg_grp.get(bi % 2))
                                V(ve, "v")
                                b.inc(ve.tensor_copy(stg, pk[bi][0:wdt, 0:PS]), "v")
                                b.wait(sy, "v")
                            stg_grp[bi % 2] = hdma(hbc_dram.ap()[cb, 0:wdt, s0_:s1_], stg,
                                                   sem="stg%d" % (bi % 2))
                        if pi == NPS - 1:
                            for bi, (cb, wdt) in enumerate(grp):
                                if cb == 34:
                                    continue
                                b.wait(ve, "t", tv)
                                b.waitg(ve, stg_grp.get("tail"))
                                V(ve, "v")
                                tst = evst[0:wdt, 2 * PS:2 * PS + 3]
                                b.inc(ve.tensor_copy(
                                    tst, tailacc[0:wdt, cb * 3:cb * 3 + 3]), "v")
                                b.wait(sy, "v")
                                stg_grp["tail"] = hdma(hbc_dram.ap()[cb, 0:wdt, T:T + 3], tst,
                                                       sem="stgt")
            hw_hbc = [("stg0", b.cnt["stg0"]), ("stg1", b.cnt["stg1"]),
                      ("stgt", b.cnt["stgt"])]
            fence()
            stop("B")

            mark("C0")
            b.wait(ve, "v")
            V(ve, "v")
            b.inc(ve.tensor_scalar(dt_fm[:, 0:S], dt_fm[:, 0:S], dtb_sb[:, 0:1], None, ALU.add), "v")
            b.wait(ac, "v")
            V(ac, "a")
            b.inc(ac.activation(spl[:, 0:S], dt_fm[:, 0:S], AF.Abs), "a")
            b.inc(ac.activation(spl[:, 0:S], spl[:, 0:S], AF.Exp, scale=-1.0), "a")
            b.inc(ac.activation(spl[:, 0:S], spl[:, 0:S], AF.Ln, bias=onef_c[0:64, 0:1]), "a")
            b.inc(ac.activation(dt_fm[:, 0:S], dt_fm[:, 0:S], AF.Relu), "a")
            b.wait(ve, "a")
            b.inc(ve.tensor_tensor(dt_fm[:, 0:S], dt_fm[:, 0:S], spl[:, 0:S], ALU.add), "v")
            b.inc(ve.tensor_scalar(spl[:, 0:S], dt_fm[:, 0:S], acol[:, 0:1], None, ALU.mult), "v")
            v_dt = b.val("v")
            if "dtf" in taps:
                b.wait(sy, "v", v_dt)
                tap_dma(taps["dtf"], dt_fm[:, :])

            mark("CD0")
            with ExitStack() as sc_:
                sbc = mk(sc_)
                s0m = sbc("s0m", [128, NCH * CH], BF16)
                b_tm = sbc("b_tm", [128, NCH * NST], BF16)
                dt_tm = sbc("dt_tm", [128, NCH * H], F32)
                ncum_tm = sbc("ncum_tm", [128, NCH * H], F32)
                s_tm = sbc("s_tm", [128, NCH * H], F32)
                cdrep = sbc("cdrep", [128, NCH * H], F32)
                prev = sbc("prev", [128, DI], BF16)
                cdrow = sbc("cdrow", [1, NCH * H], F32)
                dprow = sbc("dprow", [1, (NCH + 1) * H], F32)
                hbcr = sbc("hbcr", [128, 34 * (CH + 3)], BF16)
                xfr = sbc("xfr", [128, 32 * CH], BF16)
                bfr = sbc("bfr", [128, CH], BF16)
                xtmr = sbc("xtmr", [128, 32 * CH], BF16)
                xs2 = sbc("xs2", [128, 8 * 64], BF16)
                xd2 = sbc("xd2", [128, 8 * 64], BF16)
                er = sbc("er", [128, 512], BF16)
                e2r = sbc("e2r", [128, 512], BF16)
                wtr = sbc("wtr", [128, 8 * 128], BF16)
                hilo = sbc("hilo", [2, CH * H], BF16)
                cumf = sbc("cumf", [64, CH], F32)
                hfl = sbc("hfl", [64, CH], BF16)
                lfl = sbc("lfl", [64, CH], BF16)
                sm64a = sbc("sm64a", [64, CH], F32)
                sm64b = sbc("sm64b", [64, CH], F32)
                convs = sbc("convs", [128, 11 * CH], F32)
                rowh = sbc("rowh", [1, H], F32)

                b.wait(ve, "v", v_pre)
                V(ve, "v")
                b.inc(ve.memset(prev[:, :], 0.0), "v")
                b.inc(ve.memset(dprow[0:1, 0:H], 1.0), "v")

                for c in range(NCH):
                    if "C" in FENCE_EVERY:
                        fence()
                    tok0 = c * CH
                    o_h = c * H
                    b.wait(ve, "v", v_dt)
                    b.wait(sy, "v")
                    b.wait(sy, "t")
                    V(ve, "v")
                    b.inc(ve.tensor_tensor_scan(cumf[:, :], ones64[:, :],
                                                spl[:, 3 + tok0:3 + tok0 + CH], zcol[:, 0:1],
                                                ALU.mult, ALU.add), "v")
                    b.inc(ve.tensor_copy(cumS[:, tok0:tok0 + CH], cumf[:, :]), "v")
                    b.inc(ve.tensor_copy(hfl[:, :], cumf[:, :]), "v")
                    b.inc(ve.tensor_tensor(lfl[:, :], cumf[:, :], hfl[:, :], ALU.subtract), "v")
                    vv = b.val("v")
                    b.wait(sy, "v", vv)
                    wv_hilo = _dma("hw", [(hilo[0:1, :], hfl[:, :]),
                                          (hilo[1:2, :], lfl[:, :])])
                    b.wait(sy, "v", vv)
                    b.wait(te, "v", vv)
                    b.inc(te.transpose(pk[5][:, 0:64], cumf[:, :], identf_sb[0:64, 0:64]), "t")
                    b.wait(ve, "t")
                    b.inc(ve.tensor_scalar(ncum_tm[:, o_h:o_h + H], pk[5][:, 0:64], -1.0,
                                           None, ALU.mult), "v")
                    b.wait(te, "v")
                    b.inc(te.transpose(pk[5][:, 64:128], dt_fm[:, 3 + tok0:3 + tok0 + CH],
                                       identf_sb[0:64, 0:64]), "t")
                    b.wait(ve, "t")
                    b.inc(ve.tensor_copy(dt_tm[:, o_h:o_h + H], pk[5][:, 64:128]), "v")
                    b.wait(ac, "v", vv)
                    V(ac, "a")
                    b.inc(ac.activation(sm64a[:, :], cumf[:, :], AF.Exp, scale=-1.0,
                                        bias=cumf[:, CH - 1:CH]), "a")
                    b.wait(ve, "a")
                    b.inc(ve.tensor_tensor(sm64b[:, :], sm64a[:, :],
                                           dt_fm[:, 3 + tok0:3 + tok0 + CH], ALU.mult), "v")
                    b.wait(te, "v")
                    b.inc(te.transpose(pk[5][:, 0:64], sm64b[:, :], identf_sb[0:64, 0:64]), "t")
                    b.wait(ve, "t")
                    b.inc(ve.tensor_copy(s_tm[:, o_h:o_h + H], pk[5][:, 0:64]), "v")
                    wv_cd = hdma(rowh[0:1, :], cumf[:, CH - 1:CH])
                    b.waitg(ac, wv_cd)
                    b.inc(ac.activation(cdrow[0:1, o_h:o_h + H], rowh[0:1, :], AF.Exp), "a")
                    b.wait(ve, "a")
                    if c > 0:
                        b.inc(ve.tensor_tensor(dprow[0:1, o_h:o_h + H],
                                               dprow[0:1, o_h - H:o_h],
                                               cdrow[0:1, o_h - H:o_h], ALU.mult), "v")
                    b.inc(ve.tensor_copy(rowbf[0:1, 0:H], cdrow[0:1, o_h:o_h + H]), "v")
                    b.inc(ve.tensor_copy(rowbf[0:1, H:2 * H], dprow[0:1, o_h:o_h + H]), "v")
                    b.wait(te, "v")
                    b.inc(te.matmul(pk[5][:, 0:64], ones_sb[0:1, :], rowbf[0:1, 0:H],
                                    start=True, stop=True), "t")
                    b.inc(te.matmul(pk[5][:, 64:128], ones_sb[0:1, :], rowbf[0:1, H:2 * H],
                                    start=True, stop=True), "t")
                    b.wait(ve, "t")
                    b.inc(ve.tensor_copy(cdrep[:, o_h:o_h + H], pk[5][:, 0:64]), "v")
                    b.inc(ve.tensor_copy(dprep[:, o_h:o_h + H], pk[5][:, 64:128]), "v")
                    b.wait(te, "v")

                    b.wait(sy, "a")
                    b.wait(sy, "v")
                    for g_ in hw_hbc:
                        b.waitg(sy, g_)
                    wv = hdma(hbcr[:, :].rearrange("p (cb s) -> p cb s", cb=34),
                              hbc_dram.ap()[:, :, tok0:tok0 + CH + 3].rearrange("cb p s -> p cb s"))
                    b.waitg(ve, wv)
                    b.waitg(ac, wv)
                    W_ = CH + 3
                    av_prev = b.val("a")
                    for cb0, ncb in [(0, 11), (11, 11), (22, 10), (32, 1), (33, 1)]:
                        b.wait(ve, "a", av_prev)
                        V(ve, "v")
                        for cb in range(cb0, cb0 + ncb):
                            seg = hbcr[:, cb * W_:(cb + 1) * W_]
                            cv = convs[:, (cb - cb0) * CH:(cb - cb0 + 1) * CH]
                            b.inc(ve.tensor_scalar(cv, seg[:, 0:CH],
                                                   convw_sb[:, cb * 4:cb * 4 + 1],
                                                   convb_sb[:, cb:cb + 1],
                                                   ALU.mult, ALU.add), "v")
                            for k in range(1, 4):
                                b.inc(ve.scalar_tensor_tensor(cv, seg[:, k:k + CH],
                                                              convw_sb[:, cb * 4 + k:cb * 4 + k + 1],
                                                              cv, ALU.mult, ALU.add), "v")
                        if cb0 < 32:
                            dst = xfr[:, cb0 * CH:(cb0 + ncb) * CH]
                        elif cb0 == 32:
                            dst = bfr[:, :]
                        else:
                            dst = c_fm[:, tok0:tok0 + CH]
                        b.wait(ac, "v")
                        V(ac, "a")
                        b.inc(ac.activation(dst, convs[:, 0:ncb * CH], AF.Silu), "a")
                        av_prev = b.val("a")
                    av_conv = b.val("a")

                    for half in range(4):
                        b.wait(te, "a", av_conv)
                        b.wait(te, "v")
                        for kk in range(8):
                            cb = half * 8 + kk
                            b.inc(te.transpose(pbf[:, kk * 128:(kk + 1) * 128],
                                               xfr[:, cb * CH:(cb + 1) * CH], ident_sb[:, :]), "t")
                        b.wait(ve, "t")
                        V(ve, "v")
                        b.inc(ve.tensor_copy(xtmr[:, half * 1024:(half + 1) * 1024], pbf[:, :]), "v")
                    b.wait(te, "v")
                    b.inc(te.transpose(pbf[:, 0:128], bfr[:, :], ident_sb[:, :]), "t")
                    b.wait(ve, "t")
                    b.inc(ve.tensor_copy(b_tm[:, c * NST:(c + 1) * NST], pbf[:, 0:128]), "v")
                    b.wait(te, "a", av_conv)
                    b.wait(te, "v")
                    b.inc(te.matmul(pk[4][:, 0:128], bfr[:, :], c_fm[:, tok0:tok0 + CH],
                                    start=True, stop=True), "t")
                    b.wait(ve, "t")
                    b.inc(ve.tensor_copy(s0m[:, tok0:tok0 + CH], pk[4][:, 0:128]), "v")
                    b.wait(te, "v")
                    v_chunkprep = b.val("v")

                    b.wait(sy, "v")
                    wv_plexp = hdma(prevL_dram.ap()[c * NST:(c + 1) * NST, :], prev[:, :],
                                    sem="plst")
                    carry_waited = [False]

                    for g8 in range(8):
                        if "G8" in FENCE_EVERY:
                            fence()
                        for q4 in range(2):
                            pR = pk[q4]
                            b.waitg(te, wv_hilo)
                            b.wait(te, "a")
                            mm = None
                            for i_ in range(4):
                                h = g8 * 8 + q4 * 4 + i_
                                sl = pR[:, i_ * 128:(i_ + 1) * 128]
                                te.matmul(sl, ones2_sb[:, :], hilo[:, h * CH:(h + 1) * CH],
                                          start=True, stop=False)
                                te.matmul(sl, hilo[:, h * CH:(h + 1) * CH], nones2_sb[:, :],
                                          start=False, stop=False)
                                mm = te.matmul(sl, tri_sb[:, :], negI_sb[:, :],
                                               start=False, stop=True)
                            b.inc(mm, "t")
                            b.wait(ac, "t")
                            b.wait(ac, "v")
                            V(ac, "a")
                            b.inc(ac.activation((er if q4 == 0 else e2r)[:, :], pR[:, :], AF.Exp), "a")
                        av8 = b.val("a")
                        pY = pk[2]
                        pS = pk[3]
                        tv_prevgrp = b.val("t")
                        b.wait(ve, "a", av8)
                        b.wait(ve, "t", tv_prevgrp)
                        V(ve, "v")
                        for i in range(8):
                            h = g8 * 8 + i
                            esl = (er if i < 4 else e2r)[:, (i % 4) * 128:(i % 4 + 1) * 128]
                            b.inc(ve.scalar_tensor_tensor(wtr[:, i * 128:(i + 1) * 128], esl,
                                                          dt_tm[:, o_h + h:o_h + h + 1],
                                                          s0m[:, tok0:tok0 + CH],
                                                          ALU.mult, ALU.mult), "v")
                            b.inc(ve.tensor_scalar(xs2[:, i * 64:(i + 1) * 64],
                                                   xtmr[:, h * 64:(h + 1) * 64],
                                                   s_tm[:, o_h + h:o_h + h + 1], None, ALU.mult), "v")
                            b.inc(ve.tensor_scalar(xd2[:, i * 64:(i + 1) * 64],
                                                   xtmr[:, h * 64:(h + 1) * 64],
                                                   drep[:, h:h + 1], None, ALU.mult), "v")
                        vv = b.val("v")
                        b.wait(te, "v", vv)
                        for i in range(8):
                            h = g8 * 8 + i
                            v_ = i % 2
                            q_ = i // 2
                            ysl = pY[64 * v_:64 * v_ + 64, q_ * 128:(q_ + 1) * 128]
                            te.matmul(ysl, xtmr[:, h * 64:(h + 1) * 64],
                                      wtr[:, i * 128:(i + 1) * 128],
                                      start=True, stop=False)
                            b.inc(te.matmul(ysl, xd2[:, i * 64:(i + 1) * 64],
                                            ident_sb[:, :], start=False,
                                            stop=True), "t")
                            b.inc(te.matmul(pS[:, i * 64:(i + 1) * 64],
                                            b_tm[:, c * NST:(c + 1) * NST],
                                            xs2[:, i * 64:(i + 1) * 64],
                                            start=True, stop=True), "t")
                        tv8 = b.val("t")
                        b.wait(ve, "t", tv8)
                        V(ve, "v")
                        b.inc(ve.tensor_copy(
                            yh[:, :].rearrange("p (cb t) -> p cb t", cb=32)
                                [:, g8 * 4:(g8 + 1) * 4, tok0:tok0 + CH],
                            pY[:, :].rearrange("p (q i) -> p q i", q=4)), "v")
                        if not carry_waited[0]:
                            b.waitg(ve, wv_plexp)
                            carry_waited[0] = True
                        for i in range(8):
                            h = g8 * 8 + i
                            b.inc(ve.scalar_tensor_tensor(prev[:, h * 64:(h + 1) * 64],
                                                          prev[:, h * 64:(h + 1) * 64],
                                                          cdrep[:, o_h + h:o_h + h + 1],
                                                          pS[:, i * 64:(i + 1) * 64],
                                                          ALU.mult, ALU.add), "v")
                        b.wait(te, "v")

                b.inc(ve.tensor_tensor(dprow[0:1, NCH * H:(NCH + 1) * H],
                                       dprow[0:1, (NCH - 1) * H:NCH * H],
                                       cdrow[0:1, (NCH - 1) * H:NCH * H], ALU.mult), "v")
                b.inc(ve.tensor_copy(rowbf[0:1, 0:H], dprow[0:1, NCH * H:(NCH + 1) * H]), "v")
                b.inc(ve.memset(er[:, 0:H], 0.0), "v")
                b.wait(sy, "v")
                wv_pad = hdma(st_in.ap()[:, DI:DI + 64], er[:, 0:H])
                b.waitg(sy, wv_pad)
                wv_sti = _dma("hw", [(st_in.ap()[:, 0:DI], prev[:, :]),
                                     (st_in.ap()[0:1, DI:DI + 64], rowbf[0:1, 0:H])])
                b.waitg(gp, wv_sti)
                b.inc(gp.collective_compute("AllGather", ALU.bypass,
                                            replica_groups=[list(range(NCORES))],
                                            ins=[st_in.ap().opt()], outs=[st_out.ap().opt()]),
                      "cc")
            fence()
            if "cum" in taps:
                b.wait(sy, "v")
                tap_dma(taps["cum"], cumS[:, :])
            if "yh0" in taps:
                b.wait(sy, "v")
                tap_dma(taps["yh0"], yh[:, :])
            stop("D0")

            mark("D2")
            with ExitStack() as sd:
                sbd = mk(sd)
                inc_sb = sbd("inc_sb", [128, DI], BF16)
                sjr = sbd("sjr", [128, DI], BF16)
                ptr_ = sbd("ptr_", [128, 2 * DI], BF16)
                plr = sbd("plr", [128, 2 * DI], BF16)
                er2 = sbd("er2", [128, 512], BF16)
                e2r2 = sbd("e2r2", [128, 512], BF16)
                chat = sbd("chat", [128, 2 * 128], BF16)
                hilo2 = sbd("hilo2", [2, CH * H], BF16)
                hfl2 = sbd("hfl2", [64, CH], BF16)
                lfl2 = sbd("lfl2", [64, CH], BF16)
                whm = sbd("whm", [64, 4 * (NCORES - 1)], F32)
                rows7 = sbd("rows7", [NCORES - 1, 64], F32)
                ptj = sbd("ptj", [NCORES - 1, 64], BF16)
                wrflat = sbd("wrflat", [1, (NCORES - 1) * H], BF16)
                wrep = sbd("wrep", [128, (NCORES - 1) * H], F32)

                b.wait(sy, "cc")
                b.wait(gp, "cc")
                NJ = NCORES - 1
                deff = whm[:, 0:NJ]
                drev = whm[:, NJ:2 * NJ]
                prodv = whm[:, 2 * NJ:3 * NJ]
                pj = whm[:, 3 * NJ:4 * NJ]
                b.wait(gp, "cc")
                wv = _dma("sw", [(rows7[j:j + 1, :],
                                  st_out.ap()[j * NST:j * NST + 1, DI:DI + 64])
                                 for j in range(NJ)])
                b.waitg(te, wv)
                b.wait(te, "v")
                b.inc(te.transpose(pk[5][0:64, 0:NJ], rows7[0:NJ, :],
                                   identf_sb[0:NJ, 0:NJ]), "t")
                b.wait(ve, "t")
                V(ve, "v")
                b.inc(ve.tensor_copy(deff, pk[5][0:64, 0:NJ]), "v")
                b.inc(ve.tensor_scalar(deff, deff, -1.0, None, ALU.add), "v")
                b.inc(ve.tensor_tensor(deff, deff, selv[:, NJ:2 * NJ], ALU.mult), "v")
                b.inc(ve.tensor_scalar(deff, deff, 1.0, None, ALU.add), "v")
                for j in range(NJ):
                    b.inc(ve.tensor_copy(drev[:, j:j + 1], deff[:, NJ - 1 - j:NJ - j]), "v")
                b.inc(ve.memset(pj[:, :], 1.0), "v")
                b.inc(ve.tensor_tensor_scan(prodv, drev, pj[:, 0:NJ],
                                            pj[:, 0:1], ALU.mult, ALU.mult), "v")
                for j in range(NJ - 1):
                    b.inc(ve.tensor_copy(pj[:, j:j + 1], prodv[:, NJ - 2 - j:NJ - 1 - j]), "v")
                b.inc(ve.memset(pj[:, NJ - 1:NJ], 1.0), "v")
                b.inc(ve.tensor_tensor(pj, pj, selv[:, 0:NJ], ALU.mult), "v")
                b.inc(ve.memset(inc_sb[:, :], 0.0), "v")
                b.wait(te, "v")
                b.inc(te.transpose(pk[5][0:NJ, 64:128], pj[0:64, 0:NJ],
                                   identf_sb[0:64, 0:64]), "t")
                b.wait(ve, "t")
                b.inc(ve.tensor_copy(ptj[0:NJ, :], pk[5][0:NJ, 64:128]), "v")
                b.wait(sy, "v")
                wv_wr = _dma("hw", [(wrflat[0:1, j * H:(j + 1) * H], ptj[j:j + 1, :])
                                    for j in range(NJ)])
                b.waitg(te, wv_wr)
                for j in range(NJ):
                    b.inc(te.matmul(pk[5][:, 128 + (j % 2) * 64:192 + (j % 2) * 64],
                                    ones_sb[0:1, :], wrflat[0:1, j * H:(j + 1) * H],
                                    start=True, stop=True), "t")
                    b.wait(ve, "t")
                    b.inc(ve.tensor_copy(wrep[:, j * H:(j + 1) * H],
                                         pk[5][:, 128 + (j % 2) * 64:192 + (j % 2) * 64]), "v")
                    b.wait(te, "v")
                for j in range(NCORES - 1):
                    b.wait(sy, "v")
                    wv = hdma(sjr[:, :], st_out.ap()[j * NST:(j + 1) * NST, 0:DI])
                    b.waitg(ve, wv)
                    for h in range(H):
                        b.inc(ve.scalar_tensor_tensor(inc_sb[:, h * 64:(h + 1) * 64],
                                                      sjr[:, h * 64:(h + 1) * 64],
                                                      wrep[:, j * H + h:j * H + h + 1],
                                                      inc_sb[:, h * 64:(h + 1) * 64],
                                                      ALU.mult, ALU.add), "v")
                v_inc = b.val("v")
                if "selv" in taps:
                    b.wait(sy, "v", v_inc)
                    tap_dma(taps["selv"], selv[:, :])
                if "rows7" in taps:
                    b.wait(sy, "v", v_inc)
                    tap_dma(taps["rows7"], rows7[:, :])
                if "whm" in taps:
                    b.wait(sy, "v", v_inc)
                    tap_dma(taps["whm"], whm[:, :])
                if "wrep" in taps:
                    b.wait(sy, "v", v_inc)
                    tap_dma(taps["wrep"], wrep[:, :])
                if "incs" in taps:
                    b.wait(sy, "v", v_inc)
                    tap_dma(taps["incs"], inc_sb[:, :])

                for c in range(NCH):
                    if "D2" in FENCE_EVERY:
                        fence()
                    tok0 = c * CH
                    o_h = c * H
                    ring = c % 2
                    if c >= 2:
                        b.wait(sy, "v")
                    b.waitg(sy, ("plst", b.cnt["plst"]))
                    wv = hdma(plr[:, ring * DI:(ring + 1) * DI],
                              prevL_dram.ap()[c * NST:(c + 1) * NST, :])
                    b.wait(sy, "t")
                    V(ve, "v")
                    b.inc(ve.tensor_copy(hfl2[:, :], cumS[:, tok0:tok0 + CH]), "v")
                    b.inc(ve.tensor_tensor(lfl2[:, :], cumS[:, tok0:tok0 + CH],
                                           hfl2[:, :], ALU.subtract), "v")
                    vv = b.val("v")
                    b.wait(sy, "v", vv)
                    wv_h2 = _dma("hw", [(hilo2[0:1, :], hfl2[:, :]),
                                        (hilo2[1:2, :], lfl2[:, :])])
                    b.waitg(ve, wv)
                    b.wait(ve, "v", v_inc)
                    for h in range(H):
                        b.inc(ve.scalar_tensor_tensor(
                            ptr_[:, ring * DI + h * 64:ring * DI + (h + 1) * 64],
                            inc_sb[:, h * 64:(h + 1) * 64],
                            dprep[:, o_h + h:o_h + h + 1],
                            plr[:, ring * DI + h * 64:ring * DI + (h + 1) * 64],
                            ALU.mult, ALU.add), "v")
                    vp = b.val("v")
                    for g8 in range(8):
                        for q4 in range(2):
                            pR = pk[q4]
                            b.waitg(te, wv_h2)
                            b.wait(te, "a")
                            mm = None
                            for i_ in range(4):
                                h = g8 * 8 + q4 * 4 + i_
                                mm = te.matmul(pR[:, i_ * 128:(i_ + 1) * 128], ones2_sb[:, :],
                                               hilo2[:, h * CH:(h + 1) * CH],
                                               start=True, stop=True)
                            b.inc(mm, "t")
                            b.wait(ac, "t")
                            b.wait(ac, "v")
                            V(ac, "a")
                            b.inc(ac.activation((er2 if q4 == 0 else e2r2)[:, :], pR[:, :],
                                                AF.Exp), "a")
                        av8 = b.val("a")
                        pY = pk[2]
                        for i in range(8):
                            h = g8 * 8 + i
                            esl = (er2 if i < 4 else e2r2)[:, (i % 4) * 128:(i % 4 + 1) * 128]
                            csl = chat[:, (i % 2) * 128:(i % 2 + 1) * 128]
                            b.wait(ve, "a", av8)
                            b.wait(ve, "v", vp)
                            if i >= 2:
                                b.wait(ve, "t")
                            V(ve, "v")
                            b.inc(ve.tensor_tensor(csl, c_fm[:, tok0:tok0 + CH], esl,
                                                   ALU.mult), "v")
                            b.wait(te, "v")
                            v_ = i % 2
                            q_ = i // 2
                            b.inc(te.matmul(pY[64 * v_:64 * v_ + 64, q_ * 128:(q_ + 1) * 128],
                                            ptr_[:, ring * DI + h * 64:ring * DI + (h + 1) * 64],
                                            csl, start=True, stop=True), "t")
                        tv8 = b.val("t")
                        b.wait(ve, "t", tv8)
                        ysl = yh[:, :].rearrange("p (cb t) -> p cb t", cb=32)[
                            :, g8 * 4:(g8 + 1) * 4, tok0:tok0 + CH]
                        V(ve, "v")
                        b.inc(ve.tensor_tensor(ysl, ysl,
                                               pY[:, :].rearrange("p (q i) -> p q i", q=4),
                                               ALU.add), "v")
                        b.wait(te, "v")
            fence()
            v_yh = b.val("v")
            if "yh" in taps:
                b.wait(sy, "v", v_yh)
                tap_dma(taps["yh"], yh[:, :])
            stop("D2")

            mark("E")
            with ExitStack() as se:
                sbe = mk(se)
                wring = sbe("wringE", [128, 3 * 768], BF16)
                sgr = sbe("sgr", [128, PS], BF16)
                sq_s = sbe("sq_s", [128, T], BF16)
                rrow = sbe("rrow", [1, T], F32)
                ggroups = [list(range(i, min(i + 6, 32))) for i in range(0, 32, 6)]
                t_ring = {}
                for pi in range(NPS):
                    s0_, s1_ = pi * PS, (pi + 1) * PS
                    for gi, grp in enumerate(ggroups):
                        c0 = grp[0] * 128
                        cw = len(grp) * 128
                        b.wait(te, "v")
                        b.wait(te, "a")
                        if "E" in FENCE_EVERY:
                            fence()
                        for kc in range(KC):
                            ring = (gi * KC + kc) % 3
                            wt = wring[:, ring * 768:ring * 768 + cw]
                            if t_ring.get(ring) is not None:
                                b.wait(gp, "t", t_ring[ring])
                            wv = gdma(wt, w_gate[kc * 128:(kc + 1) * 128, c0:c0 + cw])
                            b.waitg(te, wv)
                            mm = None
                            for bi, cb in enumerate(grp):
                                mm = te.matmul(pk[bi][:, 0:PS], wt[:, bi * 128:(bi + 1) * 128],
                                               x_n1[:, kc * S + 3 + s0_:kc * S + 3 + s1_],
                                               start=(kc == 0), stop=(kc == KC - 1))
                            b.inc(mm, "t")
                            t_ring[ring] = b.val("t")
                        tv = b.val("t")
                        for bi, cb in enumerate(grp):
                            b.wait(ac, "t", tv)
                            b.wait(ac, "v")
                            V(ac, "a")
                            b.inc(ac.activation(sgr[:, 0:PS], pk[bi][:, 0:PS], AF.Silu), "a")
                            b.wait(ve, "a")
                            b.wait(ve, "v", v_yh)
                            ysl = yh[:, cb * T + s0_:cb * T + s1_]
                            V(ve, "v")
                            b.inc(ve.tensor_tensor(ysl, ysl, sgr[:, 0:PS], ALU.mult), "v")
                        b.wait(te, "v")
                v_hid = b.val("v")
                if "hidden" in taps:
                    b.wait(sy, "v", v_hid)
                    tap_dma(taps["hidden"], yh[:, :])
                for cb in range(32):
                    b.wait(ac, "v", v_hid)
                    b.wait(ac, "t")
                    V(ac, "a")
                    b.inc(ac.activation(sq_s[:, 0:T], yh[:, cb * T:(cb + 1) * T], AF.Square), "a")
                    b.wait(te, "a")
                    b.inc(te.matmul(pk[5][0:1, 0:min(512, T)], onescol_sb[:, 0:1],
                                    sq_s[:, 0:min(512, T)], start=(cb == 0), stop=(cb == 31)), "t")
                    if T > 512:
                        b.inc(te.matmul(pk[6][0:1, 0:512], onescol_sb[:, 0:1], sq_s[:, 512:1024],
                                        start=(cb == 0), stop=(cb == 31)), "t")
                b.wait(ac, "t")
                b.inc(ac.activation(rrow[0:1, 0:min(512, T)], pk[5][0:1, 0:min(512, T)], AF.Ln,
                                    scale=1.0 / DI, bias=epsln_c[0:1, 0:1]), "a")
                if T > 512:
                    b.inc(ac.activation(rrow[0:1, 512:1024], pk[6][0:1, 0:512], AF.Ln,
                                        scale=1.0 / DI, bias=epsln_c[0:1, 0:1]), "a")
                b.inc(ac.activation(rrow[0:1, 0:T], rrow[0:1, 0:T], AF.Exp, scale=-0.5), "a")
                b.wait(sy, "a")
                wv_rr = hdma(row_rr.ap()[0:1, 0:T], rrow[0:1, 0:T])
                b.waitg(sy, wv_rr)
                r_hw = hdma(rcols[:, 0:NCH],
                            row_rr.ap()[0:1, 0:T].rearrange("o (tc p) -> (o p) tc", p=128),
                            slow=True)
        fence()
        stop("E")

        with ExitStack() as s2:
            sb2 = mk(s2)
            x1 = sb2("x1", [128, NCH * D], BF16)

            mark("F")
            with ExitStack() as sf:
                sbf = mk(sf)
                wo_sb = sbf("wo_sb", [128, 16 * D], BF16)
                x0r = sbf("x0rF", [128, D], F32)
                pr_r = sbf("pr_r", [128, D], F32)
                outst = sbf("outstF", [128, D], F32)
                for p_ in range(2):
                    wvs = []
                    for kb in range(16):
                        kbg = p_ * 16 + kb
                        if p_ == 1:
                            b.wait(gp, "t")
                        wv = gdma(wo_sb[:, kb * D:(kb + 1) * D], w_out[kbg * 128:(kbg + 1) * 128, :])
                        b.waitg(ve, wv)
                        V(ve, "v")
                        b.inc(ve.tensor_scalar(wo_sb[:, kb * D:(kb + 1) * D],
                                               wo_sb[:, kb * D:(kb + 1) * D],
                                               nwc[:, kbg:kbg + 1], None, ALU.mult), "v")
                    v_w = b.val("v")
                    for tc in range(NCH):
                        if "F" in FENCE_EVERY:
                            fence()
                        tok0 = tc * CH
                        b.wait(te, "v", v_w)
                        mm = None
                        for kb in range(16):
                            for q in range(4):
                                mm = te.matmul(pk[q][:, 0:512],
                                               yh[:, (p_ * 16 + kb) * T + tok0:(p_ * 16 + kb) * T + tok0 + CH],
                                               wo_sb[:, kb * D + q * 512:kb * D + (q + 1) * 512],
                                               start=(kb == 0), stop=(kb == 15))
                        b.inc(mm, "t")
                        tv = b.val("t")
                        if p_ == 0:
                            b.wait(ve, "t", tv)
                            b.waitg(ve, ("ptst", b.cnt["ptst"]))
                            V(ve, "v")
                            for q in range(4):
                                b.inc(ve.tensor_copy(outst[:, q * 512:(q + 1) * 512],
                                                     pk[q][:, 0:512]), "v")
                            b.wait(sy, "v")
                            hdma(part_dram.ap()[tok0:tok0 + CH, :], outst[:, :], sem="ptst")
                            b.wait(te, "v")
                        else:
                            b.wait(sy, "v")
                            b.waitg(sy, ("ptst", b.cnt["ptst"]))
                            wv1 = hdma(pr_r[:, :], part_dram.ap()[tok0:tok0 + CH, :])
                            wv2 = hdma(x0r[:, :], x0[3 + tok0:3 + tok0 + CH, :])
                            b.waitg(ve, wv1)
                            b.waitg(ve, wv2)
                            b.wait(ve, "t", tv)
                            V(ve, "v")
                            for q in range(4):
                                b.inc(ve.tensor_tensor(pr_r[:, q * 512:(q + 1) * 512],
                                                       pr_r[:, q * 512:(q + 1) * 512],
                                                       pk[q][:, 0:512], ALU.add), "v")
                            b.waitg(ve, r_hw)
                            b.inc(ve.scalar_tensor_tensor(x1[:, tc * D:(tc + 1) * D], pr_r[:, :],
                                                          rcols[:, tc:tc + 1], x0r[:, :],
                                                          ALU.mult, ALU.add), "v")
                            b.wait(te, "v")
            fence()
            v_x1 = b.val("v")
            if "x1t" in taps:
                b.wait(sy, "v", v_x1)
                tap_dma(taps["x1t"], x1[:, :])
            stop("F")
            yh_cm.__exit__(None, None, None)

            mark("G")
            with ExitStack() as sg:
                sbg = mk(sg)
                rowb = sbg("rowb", [1, D], F32)
                rowc = sbg("rowc", [1, D], F32)
                thr = sbg("thr", [1, D], F32)
                fmv = sbg("fmv", [128, 16], F32)
                fmbf = sbg("fmbf", [128, 16], BF16)
                gfm = sbg("gfm", [128, 8], BF16)
                grow = sbg("grow", [1, HS], BF16)
                bcbt = sbg("bcbt", [128, D], BF16, side="right")
                ogt = sbg("ogt", [128, D], BF16, side="right")
                obt = sbg("obt", [128, D], BF16, side="right")
                sgw = ExitStack()
                sbgw = mk(sgw)
                g_ing = sbgw("g_ing", [1, D], BF16)
                g_inb = sbgw("g_inb", [1, D], BF16)
                g_png = sbgw("g_png", [1, D], BF16)
                g_pnb = sbgw("g_pnb", [1, D], BF16)
                g_ab = sbgw("g_ab", [1, D], BF16)
                g_d2b = sbgw("g_d2b", [1, D], BF16)
                aggw_sb = sbgw("aggw_sb", [128, KC * AC_], BF16)
                d1w_sb = sbgw("d1w_sb", [128, KC * HS], BF16)
                d2w_sb = sbgw("d2w_sb", [128, 8 * D], BF16)
                d1b_bf = sbgw("d1b_bf", [1, HS], BF16)

                b.wait(gp, "v", v_x1)
                wv_lt = gdma(ltok_in.ap(), x1[127:128, (NCH - 1) * D:NCH * D])
                b.waitg(gp, wv_lt)
                b.inc(gp.collective_compute("AllGather", ALU.bypass,
                                            replica_groups=[list(range(NCORES))],
                                            ins=[ltok_in.ap().opt()], outs=[ltok_out.ap().opt()]),
                      "cc")
                b.wait(sy, "cc")
                wv_rows = hdma(rowa[0:1, :], ltok_out.ap()[NCORES - 1:NCORES, :])
                wv_rows_sw = _dma("sw", [(g_ing[0:1, :], in_gamma),
                                         (g_inb[0:1, :], in_beta),
                                         (g_png[0:1, :], pn_gamma),
                                         (g_pnb[0:1, :], pn_beta),
                                         (g_ab[0:1, :], agg_b),
                                         (g_d2b[0:1, :], d2_b)])

                def ln_row(src, dst, gamma=None, beta=None, eps=EPS_FFN, heron=True):
                    epsc = epsln_c if eps == EPS_LN else epsffn_c
                    ln_acc(src, dst, 1, D, epsc, lut=not heron, eps_f=eps)
                    b.wait(ve, "a")
                    b.wait(ve, "v")
                    V(ve, "v")
                    b.inc(ve.tensor_scalar(dst, src, nm[0:1, 0:1], rsc[0:1, 0:1],
                                           ALU.add, ALU.mult), "v")
                    if gamma is not None:
                        b.inc(ve.tensor_tensor(dst, dst, gamma, ALU.mult), "v")
                        b.inc(ve.tensor_tensor(dst, dst, beta, ALU.add), "v")

                def row_to_fm(src_row, dst_fm, n):
                    b.wait(sy, "v")
                    g1 = hdma(row_rr.ap()[0:1, 0:n * 128], src_row)
                    b.waitg(sy, g1)
                    return hdma(dst_fm,
                                row_rr.ap()[0:1, 0:n * 128].rearrange("o (k p) -> (o p) k", p=128),
                                slow=True)

                b.waitg(ve, wv_rows)
                b.waitg(ve, wv_rows_sw)
                b.waitg(ac, wv_rows)
                b.waitg(ac, wv_rows_sw)
                ln_row(rowa[0:1, :], rowb[0:1, :], eps=EPS_LN, heron=False)
                ln_row(rowb[0:1, :], rowc[0:1, :], gamma=g_ing[0:1, :], beta=g_inb[0:1, :],
                       eps=EPS_FFN, heron=False)
                wv = row_to_fm(rowc[0:1, :], fmv[:, 0:16], 16)
                b.waitg(ve, wv)
                b.inc(ve.tensor_copy(fmbf[:, 0:16], fmv[:, 0:16]), "v")
                wv = gdma(aggw_sb[:, :].rearrange("p (kc n) -> p kc n", kc=KC), agg_w.rearrange("(kc p) n -> p kc n", p=128))
                b.waitg(te, wv)
                b.wait(te, "v")
                mm = None
                for kc in range(KC):
                    mm = te.matmul(pk[0][0:1, 0:AC_], fmbf[:, kc:kc + 1],
                                   aggw_sb[:, kc * AC_:(kc + 1) * AC_],
                                   start=(kc == 0), stop=(kc == KC - 1))
                b.inc(mm, "t")
                b.wait(ve, "t")
                b.inc(ve.tensor_copy(rowb[0:1, 0:AC_], pk[0][0:1, 0:AC_]), "v")
                b.wait(sy, "v")
                wv_ag = hdma(agg_io.ap(), rowb[0:1, 0:AC_])
                b.waitg(gp, wv_ag)
                b.inc(gp.collective_compute("AllGather", ALU.bypass,
                                            replica_groups=[list(range(NCORES))],
                                            ins=[agg_io.ap().opt()], outs=[agg_go.ap().opt()]),
                      "cc")
                b.wait(sy, "cc")
                wv = hdma(thr[0:1, :], agg_go.ap().rearrange("r n -> (r n)").unsqueeze(0))
                b.waitg(ve, wv)
                b.inc(ve.tensor_tensor(thr[0:1, :], thr[0:1, :], g_ab[0:1, :], ALU.add), "v")

                wv_d1 = gdma(d1w_sb[:, :].rearrange("p (kc n) -> p kc n", kc=KC), d1_w.rearrange("(kc p) n -> p kc n", p=128))
                wv_d2 = gdma(d2w_sb[:, :].rearrange("p (k2 n) -> p k2 n", k2=8), d2_w.rearrange("(k2 p) n -> p k2 n", p=128))
                wv_d1b = gdma(d1b_bf[:, :], d1_b)
                for it in range(4):
                    ln_row(thr[0:1, :], rowb[0:1, :], gamma=g_png[0:1, :], beta=g_pnb[0:1, :],
                           eps=EPS_FFN, heron=False)
                    wv = row_to_fm(rowb[0:1, :], fmv[:, 0:16], 16)
                    b.waitg(ve, wv)
                    b.inc(ve.tensor_copy(fmbf[:, 0:16], fmv[:, 0:16]), "v")
                    b.wait(te, "v")
                    b.waitg(te, wv_d1)
                    b.waitg(te, wv_d2)
                    b.waitg(te, wv_d1b)
                    for half in range(2):
                        for kc in range(KC):
                            te.matmul(pk[half][0:1, 0:512], fmbf[:, kc:kc + 1],
                                      d1w_sb[:, kc * HS + half * 512:kc * HS + (half + 1) * 512],
                                      start=(kc == 0), stop=False)
                        mm = te.matmul(pk[half][0:1, 0:512], one11[:, :],
                                       d1b_bf[0:1, half * 512:(half + 1) * 512],
                                       start=False, stop=True)
                        b.inc(mm, "t")
                    b.wait(ac, "t")
                    V(ac, "a")
                    b.inc(ac.activation(grow[0:1, 0:512], pk[0][0:1, 0:512], AF.Gelu), "a")
                    b.inc(ac.activation(grow[0:1, 512:1024], pk[1][0:1, 0:512], AF.Gelu), "a")
                    b.wait(sy, "a")
                    g1 = hdma(row_rr_bf.ap()[0:1, 0:HS], grow[0:1, :])
                    b.waitg(sy, g1)
                    wv = hdma(gfm[:, 0:8],
                              row_rr_bf.ap()[0:1, 0:HS].rearrange("o (k p) -> (o p) k", p=128),
                              slow=True)
                    b.waitg(te, wv)
                    for q in range(4):
                        mm = None
                        for k2 in range(8):
                            mm = te.matmul(pk[q + 2][0:1, 0:512], gfm[:, k2:k2 + 1],
                                           d2w_sb[:, k2 * D + q * 512:k2 * D + (q + 1) * 512],
                                           start=(k2 == 0), stop=(k2 == 7))
                        b.inc(mm, "t")
                    b.wait(ve, "t")
                    V(ve, "v")
                    for q in range(4):
                        b.inc(ve.tensor_copy(rowb[0:1, q * 512:(q + 1) * 512],
                                             pk[q + 2][0:1, 0:512]), "v")
                    b.wait(sy, "v")
                    wv_d2p = hdma(d2p_io.ap(), rowb[0:1, :])
                    b.waitg(gp, wv_d2p)
                    b.inc(gp.collective_compute("AllReduce", ALU.add,
                                                replica_groups=[list(range(NCORES))],
                                                ins=[d2p_io.ap().opt()], outs=[d2p_go.ap().opt()]),
                          "cc")
                    b.wait(sy, "cc")
                    wv = hdma(rowb[0:1, :], d2p_go.ap())
                    b.waitg(ve, wv)
                    b.inc(ve.tensor_tensor(rowb[0:1, :], rowb[0:1, :], g_d2b[0:1, :], ALU.add), "v")
                    b.inc(ve.tensor_tensor(thr[0:1, :], thr[0:1, :], rowb[0:1, :], ALU.add), "v")
                if "thought" in taps:
                    b.wait(sy, "v")
                    tap_dma(taps["thought"], thr[0:1, :])
                b.wait(gp, "t")
                wv = gdma(aggw_sb[:, :].rearrange("p (kc n) -> p kc n", kc=KC), bc_w.rearrange("(kc p) n -> p kc n", p=128))
                wv2 = row_to_fm(thr[0:1, :], fmv[:, 0:16], 16)
                b.waitg(ve, wv2)
                b.inc(ve.tensor_copy(fmbf[:, 0:16], fmv[:, 0:16]), "v")
                b.waitg(te, wv)
                b.wait(te, "v")
                mm = None
                for kc in range(KC):
                    mm = te.matmul(pk[0][0:1, 0:AC_], fmbf[:, kc:kc + 1],
                                   aggw_sb[:, kc * AC_:(kc + 1) * AC_],
                                   start=(kc == 0), stop=(kc == KC - 1))
                b.inc(mm, "t")
                b.wait(ve, "t")
                b.inc(ve.tensor_copy(rowb[0:1, 0:AC_], pk[0][0:1, 0:AC_]), "v")
                fence()
                sgw.close()
                b.wait(sy, "v")
                wv_bc = hdma(bc_io.ap(), rowb[0:1, 0:AC_])
                b.waitg(gp, wv_bc)
                b.inc(gp.collective_compute("AllGather", ALU.bypass,
                                            replica_groups=[list(range(NCORES))],
                                            ins=[bc_io.ap().opt()], outs=[bc_go.ap().opt()]),
                      "cc")
                b.wait(sy, "cc")
                wv = hdma(rowb[0:1, :], bc_go.ap().rearrange("r n -> (r n)").unsqueeze(0))
                wv2 = hdma(rowc[0:1, :], bc_b)
                b.waitg(ve, wv)
                b.waitg(ve, wv2)
                b.inc(ve.tensor_tensor(rowb[0:1, :], rowb[0:1, :], rowc[0:1, :], ALU.add), "v")
                b.inc(ve.tensor_copy(rowbf[0:1, :], rowb[0:1, :]), "v")
                for ridx, dst in [(0, bcbt), (1, ogt), (2, obt)]:
                    if ridx == 1:
                        wv3 = hdma(rowa[0:1, :], out_gamma)
                        b.waitg(ve, wv3)
                        b.inc(ve.tensor_copy(rowbf[0:1, :], rowa[0:1, :]), "v")
                    if ridx == 2:
                        wv3 = hdma(rowa[0:1, :], out_beta)
                        b.waitg(ve, wv3)
                        b.inc(ve.tensor_copy(rowbf[0:1, :], rowa[0:1, :]), "v")
                    b.wait(te, "v")
                    for q in range(4):
                        b.inc(te.matmul(pk[q][:, 0:512], ones_sb[0:1, :],
                                        rowbf[0:1, q * 512:(q + 1) * 512],
                                        start=True, stop=True), "t")
                    b.wait(ve, "t")
                    for q in range(4):
                        b.inc(ve.tensor_copy(dst[:, q * 512:(q + 1) * 512], pk[q][:, 0:512]), "v")
                    b.wait(te, "v")
                    if ridx == 1:
                        b.wait(sy, "v")

                mark("H")
                with ExitStack() as sh:
                    sbh = mk(sh)
                    gw_sb = sbh("gw_sb", [128, KC * D], BF16)
                    gsb = sbh("gsb", [128, NCH * D], BF16)
                    xftm = sbh("xftm", [128, D], BF16)
                    xffm = sbh("xffm", [128, KC * 128], BF16)
                    prh = sbh("prh", [128, D], F32)
                    outh = sbh("outh", [128, D], F32)
                    b.wait(gp, "t")
                    w_gws = _dma("sw", [(gw_sb[:, kb * D:(kb + 1) * D],
                                         gate_w[kb * 128:(kb + 1) * 128, :])
                                        for kb in range(KC)])
                    b.wait(sy, "v")
                    w_gsw = gdma(rowbf[0:1, :], gate_b)
                    for tc in range(NCH):
                        if "H" in FENCE_EVERY:
                            fence()
                        xc = x1[:, tc * D:(tc + 1) * D]
                        ln_acc(xc, xftm[:, :])
                        b.wait(ve, "a")
                        V(ve, "v")
                        b.inc(ve.tensor_scalar(xftm[:, :], xc, nm[:, 0:1], rsc[:, 0:1],
                                               ALU.add, ALU.mult), "v")
                        for half in range(2):
                            b.wait(te, "v")
                            for kk in range(8):
                                kc = half * 8 + kk
                                b.inc(te.transpose(pbf[:, kk * 128:(kk + 1) * 128],
                                                   xftm[:, kc * 128:(kc + 1) * 128],
                                                   ident_sb[:, :]), "t")
                            b.wait(ve, "t")
                            V(ve, "v")
                            b.inc(ve.tensor_copy(xffm[:, half * 1024:(half + 1) * 1024],
                                                 pbf[:, :]), "v")
                            b.wait(te, "v")
                        b.waitg(te, w_gws)
                        b.waitg(te, w_gsw)
                        b.wait(te, "v")
                        for kc in range(KC):
                            for q in range(4):
                                te.matmul(pk[q][:, 0:512], xffm[:, kc * 128:(kc + 1) * 128],
                                          gw_sb[:, kc * D + q * 512:kc * D + (q + 1) * 512],
                                          start=(kc == 0), stop=False)
                        mm = None
                        for q in range(4):
                            mm = te.matmul(pk[q][:, 0:512], ones_sb[0:1, :],
                                           rowbf[0:1, q * 512:(q + 1) * 512],
                                           start=False, stop=True)
                        b.inc(mm, "t")
                        b.wait(ac, "t")
                        V(ac, "a")
                        for q in range(4):
                            b.inc(ac.activation(gsb[:, tc * D + q * 512:tc * D + (q + 1) * 512],
                                                pk[q][:, 0:512], AF.Sigmoid), "a")
                        b.wait(te, "a")
                    if "gate" in taps:
                        b.wait(sy, "a")
                        tap_dma(taps["gate"], gsb[:, :])
                    for tc in range(NCH):
                        xc = x1[:, tc * D:(tc + 1) * D]
                        ln_acc(xc, xftm[:, :])
                        b.wait(ve, "a")
                        V(ve, "v")
                        b.inc(ve.tensor_scalar(xftm[:, :], xc, nm[:, 0:1], rsc[:, 0:1],
                                               ALU.add, ALU.mult), "v")
                        b.inc(ve.tensor_tensor(prh[:, :], gsb[:, tc * D:(tc + 1) * D],
                                               bcbt[:, :], ALU.mult), "v")
                        b.inc(ve.tensor_tensor(prh[:, :], prh[:, :], xftm[:, :], ALU.add), "v")
                        ln_acc(prh[:, :], xftm[:, :], epsc=epsffn_c)
                        b.wait(ve, "a")
                        V(ve, "v")
                        b.inc(ve.tensor_scalar(prh[:, :], prh[:, :], nm[:, 0:1], rsc[:, 0:1],
                                               ALU.add, ALU.mult), "v")
                        b.inc(ve.tensor_tensor(prh[:, :], prh[:, :], ogt[:, :], ALU.mult), "v")
                        b.inc(ve.tensor_tensor(prh[:, :], prh[:, :], obt[:, :], ALU.add), "v")
                        sto = outh[:, 0:D]
                        b.waitg(ve, ("outst", b.cnt["outst"]))
                        b.inc(ve.tensor_tensor(sto, prh[:, :], xc, ALU.add), "v")
                        b.wait(sy, "v")
                        hdma(out[tc * CH:(tc + 1) * CH, :], sto, sem="outst")
            mark(None)
            b.wait(sy, "outst", b.cnt["outst"])

    return nc


def _stage_inputs(I, T=1024):
    import ml_dtypes
    f = np.float32
    bf = ml_dtypes.bfloat16
    x = np.asarray(I["L0_attn_hidden_states"], f).reshape(-1, D)
    ipw = np.asarray(I["L0_attn_in_proj_w"], f)
    w_hbcdt = np.ascontiguousarray(ipw[:, DI:DI + HBCW])
    w_gate = np.ascontiguousarray(ipw[:, 0:DI])
    w_out = np.ascontiguousarray(np.asarray(I["L0_attn_out_proj_w"], f))
    cw = np.asarray(I["L0_attn_conv_w"], f)
    conv_w = np.ascontiguousarray(cw.reshape(34, 128, 4).transpose(1, 0, 2).reshape(128, 136))
    conv_b = np.ascontiguousarray(np.asarray(I["L0_attn_conv_b"], f).reshape(34, 128).T)
    dt_bias = np.asarray(I["L0_attn_dt_bias"], f).reshape(64, 1)
    a_log = np.asarray(I["L0_attn_A_log"], f).reshape(64, 1)
    d_row = np.asarray(I["L0_attn_D_param"], f).reshape(1, H)
    norm_w = np.ascontiguousarray(np.asarray(I["L0_attn_norm_w"], f).reshape(32, 128).T)
    row = lambda k: np.ascontiguousarray(np.asarray(I[k], f).reshape(1, -1))
    n1 = np.arange(128)
    consts = {
        "ones_r": np.ones((1, 128), bf),
        "ones2_r": np.ones((2, 128), bf),
        "nones2_r": -np.ones((2, 128), bf),
        "tri_r": (n1[:, None] < n1[None, :]).astype(bf),
        "negI_r": (-1e30 * np.eye(128)).astype(bf),
        "ident_r": np.eye(128, dtype=bf),
        "identf_r": np.eye(128, dtype=f),
        "onescol_r": np.ones((128, 1), bf),
    }
    HSn = HFF // NCORES
    ACn = D // NCORES
    in_maps = []
    for k in range(NCORES):
        t0 = k * T
        xs = np.zeros((T + 3, D), f)
        lo = max(0, t0 - 3)
        xs[3 - (t0 - lo):3] = x[lo:t0]
        xs[3:] = x[t0:t0 + T]
        j = np.arange(NCORES - 1)
        m = {
            "x0": xs, "w_hbcdt": w_hbcdt, "w_gate": w_gate, "w_out": w_out,
            "conv_w": conv_w, "conv_b": conv_b, "dt_bias": dt_bias,
            "a_log": a_log, "d_row": d_row, "norm_w": norm_w,
            "in_gamma": row("L0_ffn_in_gamma"), "in_beta": row("L0_ffn_in_beta"),
            "agg_w": np.ascontiguousarray(np.asarray(I["L0_ffn_agg_w"], f)[:, k * ACn:(k + 1) * ACn]),
            "agg_b": row("L0_ffn_agg_b"),
            "d1_w": np.ascontiguousarray(np.asarray(I["L0_ffn_d1_w"], f)[:, k * HSn:(k + 1) * HSn]),
            "d1_b": np.ascontiguousarray(row("L0_ffn_d1_b")[:, k * HSn:(k + 1) * HSn]),
            "d2_w": np.ascontiguousarray(np.asarray(I["L0_ffn_d2_w"], f)[k * HSn:(k + 1) * HSn, :]),
            "d2_b": row("L0_ffn_d2_b"),
            "pn_gamma": row("L0_ffn_pn_gamma"), "pn_beta": row("L0_ffn_pn_beta"),
            "bc_w": np.ascontiguousarray(np.asarray(I["L0_ffn_bc_w"], f)[:, k * ACn:(k + 1) * ACn]),
            "bc_b": row("L0_ffn_bc_b"),
            "gate_w": np.ascontiguousarray(np.asarray(I["L0_ffn_gate_w"], f)),
            "gate_b": row("L0_ffn_gate_b"),
            "out_gamma": row("L0_ffn_out_gamma"), "out_beta": row("L0_ffn_out_beta"),
            "sel2_r": np.ascontiguousarray(np.broadcast_to(
                np.concatenate([(j < k).astype(f), (j <= k - 1).astype(f)]),
                (64, 2 * (NCORES - 1)))),
        }
        m.update(consts)
        in_maps.append(m)
    return in_maps


def kernel(**inputs):
    T = 1024
    in_maps = _stage_inputs(inputs, T)
    nc = build_program(T)
    res = run_bass_kernel_spmd(nc, in_maps, core_ids=list(range(NCORES)))
    outp = np.concatenate([np.asarray(r["out"], np.float32) for r in res.results], axis=0)
    return outp.reshape(1, NCORES * T, D)
